# version 15
# baseline (speedup 1.0000x reference)
import math
from contextlib import ExitStack

import numpy as np

import concourse.bass as bass
import concourse.mybir as mybir
from concourse.bass_utils import run_bass_kernel_spmd

F32 = mybir.dt.float32
BF16 = mybir.dt.bfloat16
I32 = mybir.dt.int32
AF = mybir.ActivationFunctionType
ALU = mybir.AluOpType
AX = mybir.AxisListType

ENGS = ("tensor", "vector", "scalar", "gpsimd", "sync")
ESEM = {"tensor": "eT", "vector": "eV", "scalar": "eS", "gpsimd": "eG", "sync": "eY"}

FULL_CFG = dict(C=2048, HEADS=32, NE=8, I=1408, SIH=1408)
N_TOK = 2048
GT = 1024
REAL = 1032
GW = 1040
WIN = 144
NQT = GT // 128
NTT = GT // 128
NLT = (REAL + 127) // 128


def make_masks(hd):
    scale = hd ** -0.5
    mc = np.zeros((3, 128, WIN), np.float32)
    ma = np.full((3, 128, WIN), -1e30, np.float32)
    for p in range(128):
        for jp in range(p, p + 16):
            d = abs(p + 8 - jp)
            mc[0, p, jp] = scale * (1 - d / 16.0)
            ma[0, p, jp] = 0.0
        for jp in range(max(0, p - 8), p + 8):
            d = abs(p - jp)
            mc[1, p, jp] = scale * (1 - d / 16.0)
            ma[1, p, jp] = 0.0
        for jp in range(p, min(p + 16, 136)):
            d = abs(p + 8 - jp)
            mc[2, p, jp] = scale * (1 - d / 16.0)
            ma[2, p, jp] = 0.0
    return mc, ma


class Buf:
    __slots__ = ("w", "r")

    def __init__(self):
        self.w = None
        self.r = []


class Bld:
    def __init__(self):
        self.nc = bass.Bass("TRN2", target_bir_lowering=False)
        self.stack = ExitStack()
        self.plan = {e: [] for e in ENGS}
        self.sems = {}
        self.cnt = {}
        self.wm = {}

    def sem(self, name):
        if name not in self.sems:
            self.sems[name] = self.stack.enter_context(self.nc.semaphore(name))
            self.cnt[name] = 0
        return self.sems[name]

    def op(self, eng, fn, rd=(), wr=(), war=(), rd_nt=(), inc=True, dma=None):
        waits = {}

        def need(tok):
            if tok is not None and tok[1] > waits.get(tok[0], 0):
                waits[tok[0]] = tok[1]

        for b in rd:
            need(b.w)
        for b in rd_nt:
            need(b.w)
        for b in war:
            need(b.w)
            for t in b.r:
                need(t)
        for b in wr:
            need(b.w)
            for t in b.r:
                need(t)
        real = []
        for s, v in waits.items():
            k = (eng, s)
            if dma is not None or self.wm.get(k, 0) < v:
                real.append((self.sem(s), v))
                if self.wm.get(k, 0) < v:
                    self.wm[k] = v
        tok = None
        if dma is not None:
            self.sem(dma)
            self.cnt[dma] += 16
            tok = (dma, self.cnt[dma])
        elif inc:
            s = ESEM[eng]
            self.sem(s)
            self.cnt[s] += 1
            tok = (s, self.cnt[s])
        self.plan[eng].append((fn, real, self.sems[tok[0]] if tok else None, bool(dma)))
        if tok is not None:
            for b in wr:
                b.w = tok
                b.r = []
            for b in rd:
                b.r.append(tok)
        else:
            assert not wr and not rd
        return tok

    def fence(self, outgoing, incoming):
        fl = [(s, v) for s, v in self.cnt.items() if v > 0]
        for b in incoming:
            b.r = b.r + [t for t in fl if t != b.w]

    def build_block(self):
        with self.nc.Block() as block:
            def run(eng_obj, lst):
                for fn, waits, incsem, isdma in lst:
                    for s, v in waits:
                        eng_obj.wait_ge(s, v)
                    ins = fn(eng_obj)
                    if incsem is not None:
                        ins.then_inc(incsem, 16 if isdma else 1)
            block.tensor(lambda e: run(e, self.plan["tensor"]))
            block.vector(lambda e: run(e, self.plan["vector"]))
            block.scalar(lambda e: run(e, self.plan["scalar"]))
            block.gpsimd(lambda e: run(e, self.plan["gpsimd"]))
            block.sync(lambda e: run(e, self.plan["sync"]))


def build_program(cfg):
    C, HEADS, NE, I, SIH = cfg["C"], cfg["HEADS"], cfg["NE"], cfg["I"], cfg["SIH"]
    HD = C // HEADS
    assert HD == 64
    CT = C // 128
    HP = HEADS // 2
    IT = I // 128
    SIT = SIH // 128
    NCH = math.ceil(C / 512)
    NBN = math.ceil(C / 512)
    b = Bld()
    nc = b.nc
    op = b.op

    x = nc.dram_tensor("x", [N_TOK, C], F32, kind="ExternalInput").ap()
    qkvw = nc.dram_tensor("qkvw", [HP, C, 384], F32, kind="ExternalInput").ap()
    projw = nc.dram_tensor("projw", [C, C], F32, kind="ExternalInput").ap()
    rw = nc.dram_tensor("rw", [C, NE], F32, kind="ExternalInput").ap()
    gw = nc.dram_tensor("gw", [NE, C, I], F32, kind="ExternalInput").ap()
    uwt = nc.dram_tensor("uw", [NE, C, I], F32, kind="ExternalInput").ap()
    dwt = nc.dram_tensor("dw", [NE, I, C], F32, kind="ExternalInput").ap()
    sgw = nc.dram_tensor("sgw", [C, 2 * SIH], F32, kind="ExternalInput").ap()
    suw = nc.dram_tensor("suw", [C, 2 * SIH], F32, kind="ExternalInput").ap()
    sdw = nc.dram_tensor("sdw", [2 * SIH, C], F32, kind="ExternalInput").ap()
    pbd = nc.dram_tensor("pb", [C], F32, kind="ExternalInput").ap()
    sgbd = nc.dram_tensor("sgb", [2 * SIH], F32, kind="ExternalInput").ap()
    subd = nc.dram_tensor("sub", [2 * SIH], F32, kind="ExternalInput").ap()
    sdbd = nc.dram_tensor("sdb", [C], F32, kind="ExternalInput").ap()
    mcd = nc.dram_tensor("mc", [3, 128, WIN], F32, kind="ExternalInput").ap()
    mad = nc.dram_tensor("ma", [3, 128, WIN], F32, kind="ExternalInput").ap()
    idd = nc.dram_tensor("ident", [128, 128], F32, kind="ExternalInput").ap()
    ltrid = nc.dram_tensor("ltri", [128, 128], F32, kind="ExternalInput").ap()
    iotad = nc.dram_tensor("iota8", [128, NE], F32, kind="ExternalInput").ap()
    y = nc.dram_tensor("y", [N_TOK, C], F32, kind="ExternalOutput").ap()
    hdbg = nc.dram_tensor("hdbg", [128, 4096], BF16, kind="ExternalOutput").ap() if cfg.get("dbg5") else None
    gdbg = nc.dram_tensor("gdbg", [128, 12288], BF16, kind="ExternalOutput").ap() if cfg.get("dbg5") else None
    CAP = 512
    xtm = nc.dram_tensor("xtm", [REAL, C], BF16).ap()
    xdram = nc.dram_tensor("xdram", [NE * CAP, C], BF16).ap()
    ddram = nc.dram_tensor("ddram", [NE * CAP, C], BF16).ap()

    A = nc.alloc_sbuf_tensor
    xnT = A("xnT", [128, CT, GW], BF16)
    accT = A("accT", [128, NTT, C], F32)
    mcs = A("mcs", [128, 3, WIN], F32)
    mas = A("mas", [128, 3, WIN], F32)
    idb = A("idb", [128, 128], BF16)
    idf = A("idf", [128, 128], F32)
    ones1 = A("ones1", [33, 128], BF16)
    brow = A("brow", [33, C], BF16)
    pbrow = brow[0:1, :]
    sdbrow = brow[32:33, :]
    sgbC = A("sgbC", [128, 2 * SIT], F32)
    subC = A("subC", [128, 2 * SIT], F32)
    rwS = A("rwS", [128, CT, NE], BF16)
    cmbT = A("cmbT", [128, NTT, NE], F32)
    rsc = A("rsc", [128, NTT, 8 * NE + 4], F32)
    epsb = A("epsb", [128, 1], F32)
    ltriS = A("ltriS", [128, 128], BF16)
    alloneS = A("alloneS", [128, 128], BF16)
    iotaS = A("iotaS", [128, NE], F32)
    indb = A("indb", [128, NTT, NE], BF16)
    rank8 = A("rank8", [128, NTT, NE], F32)
    rtmp = A("rtmp", [128, NTT, NE], F32)
    rsx = A("rsx", [128, NTT, 8], F32)
    oidx = A("oidx", [128, NTT, 2], I32)
    aidx = A("aidx", [128, NTT, 2], I32)
    c12 = A("c12", [128, NTT, 2], F32)
    bno = A("bno", [128, 2, 6 * NBN], F32)
    mv = A("mv", [128, 2, 2], F32)
    lns = A("lns", [128, 2, 4], F32)

    NGU = 6
    LN_SPECS = [
        ("xt_", [128, 2, C], F32), ("xns", [128, 2, C], BF16),
        ("qkvS", [128, 1, CT, 384], BF16), ("prjS", [128, 6, C], BF16),
        ("Qf", [128, GW], BF16), ("Kf", [128, GW], BF16), ("Vf", [128, GW], BF16),
        ("ahp", [128, 8, GT], BF16), ("scm", [128, 3, WIN], F32),
        ("exv", [128, 4, WIN], F32), ("sst", [128, 4, 4], F32),
        ("abf", [128, 4, WIN], BF16),
        ("aTa", [128, 4, 128], BF16), ("aTb", [128, 4, 128], BF16),
        ("vwa", [128, 4, 128], BF16), ("vwb", [128, 4, 128], BF16),
    ]
    NDW = 5
    MOE_SPECS = [
        ("guS", [128, NGU, CT, 128], BF16), ("dwS", [128, NDW, C], BF16),
        ("hS", [128, max(IT, SIT), 512], BF16), ("sgS", [128, 2, 512], BF16),
        ("xe", [128, 2, C], BF16), ("xeT", [128, CT, 512], BF16),
        ("stg", [128, 2, C], BF16),
    ]

    def _sz(specs):
        tot = 0
        for _, shape, dt in specs:
            n = 1
            for s in shape[1:]:
                n *= s
            tot += n if dt == F32 else (n + 1) // 2
        return tot

    ARE = max(_sz(LN_SPECS), _sz(MOE_SPECS)) + 16
    arena = A("arena", [128, ARE], F32)
    ar_f32 = arena.ap()
    ar_bf = arena.ap().bitcast(BF16)

    def carve(specs):
        out = {}
        off = 0
        for name, shape, dt in specs:
            n = 1
            for s in shape[1:]:
                n *= s
            if dt == F32:
                apv = ar_f32[:, off:off + n]
                off += n
            else:
                apv = ar_bf[:, 2 * off:2 * off + n]
                off += (n + 1) // 2
            if len(shape) == 3:
                apv = apv.rearrange("p (a b) -> p a b", a=shape[1])
            elif len(shape) == 4:
                apv = apv.rearrange("p (a b c) -> p a b c", a=shape[1], b=shape[2])
            out[name] = apv
        return out

    V1 = carve(LN_SPECS)
    V2 = carve(MOE_SPECS)
    xt_, xns, qkvS, prjS = V1["xt_"], V1["xns"], V1["qkvS"], V1["prjS"]
    Qf, Kf, Vf, ahp = V1["Qf"], V1["Kf"], V1["Vf"], V1["ahp"]
    scm, exv, sst = V1["scm"], V1["exv"], V1["sst"]
    aTa, aTb, vwa, vwb = V1["aTa"], V1["aTb"], V1["vwa"], V1["vwb"]
    abf = V1["abf"]
    guS, dwS, hS, sgS = V2["guS"], V2["dwS"], V2["hS"], V2["sgS"]
    xe, xeT, stg = V2["xe"], V2["xeT"], V2["stg"]

    pP = [nc.alloc_psum_tensor(f"pP{k}", [128, 512], F32) for k in range(8)]
    pB = [Buf() for _ in range(8)]

    def pbf(k):
        return pP[k].bitcast(BF16)

    bxnT = [Buf() for _ in range(CT)]
    bacc = [Buf() for _ in range(NTT)]
    bconst = Buf()
    bcmb = [Buf() for _ in range(NTT)]
    brr = [[Buf() for _ in range(12)] for _ in range(NTT)]
    bxt = [Buf() for _ in range(2)]
    bxns = [Buf() for _ in range(2)]
    bbno = [Buf() for _ in range(2)]
    bmv = [Buf() for _ in range(2)]
    blns = [Buf() for _ in range(2)]
    bqkv = [Buf() for _ in range(1)]
    bprj = [Buf() for _ in range(6)]
    bQ, bK, bV = Buf(), Buf(), Buf()
    baTa = [Buf() for _ in range(4)]
    baTb = [Buf() for _ in range(4)]
    bvwa = [Buf() for _ in range(4)]
    bvwb = [Buf() for _ in range(4)]
    bscm = [Buf() for _ in range(3)]
    babf = [Buf() for _ in range(4)]
    bexv = [Buf() for _ in range(4)]
    bsst = [Buf() for _ in range(4)]
    bahp = [Buf() for _ in range(8)]
    bgu = [Buf() for _ in range(NGU)]
    bdw = [Buf() for _ in range(NDW)]
    bh = [Buf() for _ in range(max(IT, SIT))]
    bsg = [Buf() for _ in range(2)]
    bxe = [Buf() for _ in range(2)]
    bxeT = Buf()
    bstg = [Buf() for _ in range(2)]
    bxtm = Buf()
    bxdram = Buf()
    bddram = Buf()
    brk = [[Buf() for _ in range(16)] for _ in range(NTT)]

    LNATTN_BUFS = [bQ, bK, bV] + bxt + bxns + bqkv + bprj + baTa + baTb + bvwa + \
        bvwb + bscm + bexv + bsst + bahp + bbno + bmv + blns
    MOE_BUFS = bgu + bdw + bh + bsg + bxe + [bxeT] + bstg

    op("gpsimd", lambda g: g.dma_start(mcs[:, :, :], mcd.rearrange("m p w -> p m w")), wr=[bconst], dma="dC")
    op("gpsimd", lambda g: g.dma_start(mas[:, :, :], mad.rearrange("m p w -> p m w")), wr=[bconst], dma="dC")
    op("gpsimd", lambda g: g.dma_start(idb[:, :], idd[:, :]), wr=[bconst], dma="dC")
    op("gpsimd", lambda g: g.dma_start(idf[:, :], idd[:, :]), wr=[bconst], dma="dC")
    op("gpsimd", lambda g: g.dma_start(ltriS[:, :], ltrid[:, :]), wr=[bconst], dma="dC")
    op("gpsimd", lambda g: g.dma_start(iotaS[:, :], iotad[:, :]), wr=[bconst], dma="dC")
    op("gpsimd", lambda g: g.memset(alloneS[:, :], 1.0), wr=[bconst])
    op("gpsimd", lambda g: g.dma_start(pbrow[0:1, :], pbd[None, :]), wr=[bconst], dma="dC")
    op("gpsimd", lambda g: g.dma_start(sdbrow[0:1, :], sdbd[None, :]), wr=[bconst], dma="dC")
    op("gpsimd", lambda g: g.dma_start(sgbC[:, :], sgbd.rearrange("(t p) -> p t", p=128),
                                       allow_slow_non_contiguous=True), wr=[bconst], dma="dC")
    op("gpsimd", lambda g: g.dma_start(subC[:, :], subd.rearrange("(t p) -> p t", p=128),
                                       allow_slow_non_contiguous=True), wr=[bconst], dma="dC")
    op("gpsimd", lambda g: g.dma_start(rwS[:, :, :], rw.rearrange("(ct p) e -> p ct e", p=128)),
       wr=[bconst], dma="dC")
    op("gpsimd", lambda g: g.memset(epsb[:, :], 1e-5), wr=[bconst])
    op("gpsimd", lambda g: g.memset(ones1[:, :], 1.0), wr=[bconst])

    op("vector", lambda e: e.memset(xe[:, 0, :], 0.0), wr=[bxe[0]])
    for r0 in range(0, NE * CAP, 128):
        op("sync", lambda e, r0=r0: e.dma_start(
            xdram[r0:r0 + 128, :], xe[:, 0, :]), rd=[bxe[0]], wr=[bxdram], dma="dZ")
    b.fence([bxe[0], bxdram], LNATTN_BUFS)

    for grp in range(1 if cfg.get("dbg3") else 2):
        s_g = 0 if grp == 0 else GT - 8
        own0 = 0 if grp == 0 else 8

        if grp > 0:
            b.fence(MOE_BUFS, LNATTN_BUFS)

        def ln_stats(lt):
            rows = 128 if lt < NLT - 1 else REAL - 128 * (NLT - 1)
            t0 = s_g + 128 * lt
            p = lt % 2
            op("sync", lambda e, p=p, t0=t0, rows=rows: e.dma_start(
                xt_[0:rows, p, :], x[t0:t0 + rows, :]), wr=[bxt[p]], dma=f"dX{p}")
            for k in range(NBN):
                cl = min(512, C - 512 * k)
                op("vector", lambda e, p=p, rows=rows, k=k, cl=cl: e.bn_stats(
                    bno[0:rows, p, 6 * k:6 * k + 6], xt_[0:rows, p, 512 * k:512 * k + cl]),
                    rd=[bxt[p]], wr=[bbno[p]])
            op("vector", lambda e, p=p, rows=rows: e.bn_aggr(
                mv[0:rows, p, :], bno[0:rows, p, :]), rd=[bbno[p]], wr=[bmv[p]])
            op("scalar", lambda e, p=p, rows=rows: e.activation(
                lns[0:rows, p, 0:1], mv[0:rows, p, 1:2], AF.Sqrt, bias=epsb[0:rows, 0:1]),
                rd=[bmv[p], bconst], wr=[blns[p]])
            op("vector", lambda e, p=p, rows=rows: e.reciprocal(
                lns[0:rows, p, 1:2], lns[0:rows, p, 0:1]), rd=[blns[p]], wr=[blns[p]])
            op("vector", lambda e, p=p, rows=rows: e.tensor_scalar(
                lns[0:rows, p, 2:3], mv[0:rows, p, 0:1], lns[0:rows, p, 1:2], -1.0,
                ALU.mult, ALU.mult), rd=[bmv[p]], wr=[blns[p]])
            op("scalar", lambda e, p=p, rows=rows: e.activation(
                xns[0:rows, p, :], xt_[0:rows, p, :], AF.Identity,
                bias=lns[0:rows, p, 2:3], scale=lns[0:rows, p, 1:2]),
                rd=[blns[p], bxt[p]], wr=[bxns[p]])
            op("sync", lambda e, p=p, lt=lt, rows=rows: e.dma_start(
                xtm[128 * lt:128 * lt + rows, :], xns[0:rows, p, :]),
                rd=[bxns[p]], wr=[bxtm], dma="dXT")

        def ln_trans(lt):
            rows = 128 if lt < NLT - 1 else REAL - 128 * (NLT - 1)
            p = lt % 2
            col0 = 128 * lt
            for ct in range(CT):
                kb = 6 + (ct % 2)
                op("tensor", lambda e, p=p, ct=ct, rows=rows, kb=kb: e.transpose(
                    pbf(kb)[0:128, 0:rows], xns[0:rows, p, 128 * ct:128 * ct + 128],
                    idb[0:rows, 0:rows]),
                    rd=[bxns[p], bconst], war=[pB[kb]], wr=[pB[kb]])
                op("vector", lambda e, ct=ct, rows=rows, kb=kb, col0=col0: e.tensor_copy(
                    xnT[:, ct, col0:col0 + rows], pbf(kb)[0:128, 0:rows]),
                    rd=[pB[kb]], wr=[bxnT[ct]])

        for tt in range(NTT):
            t0 = grp * GT + 128 * tt
            op("scalar", lambda e, tt=tt, t0=t0: e.dma_start(
                accT[:, tt, :], x[t0:t0 + 128, :]), wr=[bacc[tt]], dma=f"dA{tt}")
        for lt in range(NLT + 1):
            if lt < NLT:
                ln_stats(lt)
            if lt >= 1:
                ln_trans(lt - 1)
        for ct in range(CT):
            op("vector", lambda e, ct=ct: e.memset(xnT[:, ct, REAL:GW], 0.0), wr=[bxnT[ct]])

        if cfg.get("dbg5") and grp == 0:
            op("sync", lambda e_: e_.dma_start(
                gdbg[:, 4096:4096 + 512], xnT[:, 0, 0:512]), rd=[bxnT[0]], dma="dHH")
        def emit_routing():
            for tt in range(NTT):
                kb = tt % 2
                oc0 = own0 + 128 * tt
                for ct in range(CT):
                    last = ct == CT - 1
                    op("tensor", lambda e, ct=ct, kb=kb, oc0=oc0: e.matmul(
                        pP[kb][:, 0:NE], xnT[:, ct, oc0:oc0 + 128], rwS[:, ct, :],
                        start=(ct == 0), stop=(ct == CT - 1)),
                        rd=[bxnT[ct], bconst] if last else (),
                        wr=[pB[kb]] if last else (),
                        rd_nt=() if last else (bxnT[ct], bconst),
                        war=(pB[kb],) if ct == 0 else (),
                        inc=last)
                op("vector", lambda e, tt=tt, kb=kb: e.tensor_copy(
                    rsc[:, tt, 0:NE], pP[kb][:, 0:NE]), rd=[pB[kb]], wr=[brr[tt][0]])
            E = NE

            def rr(tt, j):
                return rsc[:, tt, j * E:(j + 1) * E]

            def rs(tt, j):
                return rsc[:, tt, 8 * E + j:8 * E + j + 1]
            for tt in range(NTT):
                op("vector", lambda e, tt=tt: e.reduce_max(
                    rs(tt, 0), rr(tt, 0), axis=AX.X, negate=True), rd=[brr[tt][0]], wr=[brr[tt][1]])
            for tt in range(NTT):
                op("vector", lambda e, tt=tt: e.tensor_scalar_add(
                    rr(tt, 1), rr(tt, 0), rs(tt, 0)), rd=[brr[tt][0], brr[tt][1]], wr=[brr[tt][2]])
            for tt in range(NTT):
                op("scalar", lambda e, tt=tt: e.activation(
                    rr(tt, 2), rr(tt, 1), AF.Exp), rd=[brr[tt][2]], wr=[brr[tt][3]])
            for tt in range(NTT):
                op("vector", lambda e, tt=tt: e.tensor_scalar(
                    rr(tt, 3), rr(tt, 1), 0.0, None, ALU.is_equal), rd=[brr[tt][2]], wr=[brr[tt][4]])
            for tt in range(NTT):
                op("vector", lambda e, tt=tt: e.scalar_tensor_tensor(
                    rr(tt, 4), rr(tt, 3), -1e30, rr(tt, 1), ALU.mult, ALU.add),
                    rd=[brr[tt][2], brr[tt][4]], wr=[brr[tt][5]])
            for tt in range(NTT):
                op("vector", lambda e, tt=tt: e.reduce_max(
                    rs(tt, 1), rr(tt, 4), axis=AX.X), rd=[brr[tt][5]], wr=[brr[tt][6]])
            for tt in range(NTT):
                op("vector", lambda e, tt=tt: e.tensor_scalar(
                    rr(tt, 5), rr(tt, 1), rs(tt, 1), None, ALU.is_equal),
                    rd=[brr[tt][2], brr[tt][6]], wr=[brr[tt][7]])
            for tt in range(NTT):
                op("vector", lambda e, tt=tt: e.tensor_add(
                    rr(tt, 6), rr(tt, 3), rr(tt, 5)), rd=[brr[tt][4], brr[tt][7]], wr=[brr[tt][8]])
            for tt in range(NTT):
                op("vector", lambda e, tt=tt: e.tensor_mul(
                    rr(tt, 7), rr(tt, 2), rr(tt, 6)), rd=[brr[tt][3], brr[tt][8]], wr=[brr[tt][9]])
            for tt in range(NTT):
                op("vector", lambda e, tt=tt: e.reduce_sum(
                    rs(tt, 2), rr(tt, 7), axis=AX.X), rd=[brr[tt][9]], wr=[brr[tt][10]])
            for tt in range(NTT):
                op("vector", lambda e, tt=tt: e.reciprocal(
                    rs(tt, 3), rs(tt, 2)), rd=[brr[tt][10]], wr=[brr[tt][11]])
            for tt in range(NTT):
                op("vector", lambda e, tt=tt: e.tensor_scalar_mul(
                    cmbT[:, tt, :], rr(tt, 7), rs(tt, 3)),
                    rd=[brr[tt][9], brr[tt][11]], wr=[bcmb[tt]])

            for tt in range(NTT):
                op("vector", lambda e, tt=tt: e.tensor_copy(
                    indb[:, tt, :], rr(tt, 6)), rd=[brr[tt][8]], wr=[brk[tt][0]])
            for tt in range(NTT):
                kb = 2 + tt % 2
                for tt2 in range(tt):
                    op("tensor", lambda e, tt2=tt2, kb=kb, first=(tt2 == 0): e.matmul(
                        pP[kb][:, 0:NE], alloneS[:, :], indb[:, tt2, :],
                        start=first, stop=False),
                        rd_nt=(bconst, brk[tt2][0]),
                        war=(pB[kb],) if tt2 == 0 else (), inc=False)
                op("tensor", lambda e, tt=tt, kb=kb, first=(tt == 0): e.matmul(
                    pP[kb][:, 0:NE], ltriS[:, :], indb[:, tt, :],
                    start=first, stop=True),
                    rd=[bconst] + [brk[t2][0] for t2 in range(tt + 1)],
                    war=(pB[kb],) if tt == 0 else (), wr=[pB[kb]])
                op("vector", lambda e, tt=tt, kb=kb: e.tensor_copy(
                    rank8[:, tt, :], pP[kb][:, 0:NE]), rd=[pB[kb]], wr=[brk[tt][1]])
            BIGOFF = float(NE * CAP * 16)
            for tt in range(NTT):
                for j in range(2):
                    EQ = rr(tt, 3) if j == 0 else rr(tt, 5)
                    EQb = brr[tt][4] if j == 0 else brr[tt][7]
                    op("vector", lambda e, tt=tt, EQ=EQ: e.tensor_mul(
                        rtmp[:, tt, :], EQ, iotaS[:, :]), rd=[EQb, bconst], wr=[brk[tt][2]])
                    op("vector", lambda e, tt=tt, j=j: e.reduce_sum(
                        rsx[:, tt, 0 + j:1 + j], rtmp[:, tt, :], axis=AX.X),
                        rd=[brk[tt][2]], wr=[brk[tt][3 + j]])
                    op("vector", lambda e, tt=tt, EQ=EQ: e.tensor_mul(
                        rtmp[:, tt, :], EQ, rank8[:, tt, :]),
                        rd=[EQb, brk[tt][1]], wr=[brk[tt][2]])
                    op("vector", lambda e, tt=tt, j=j: e.reduce_sum(
                        rsx[:, tt, 2 + j:3 + j], rtmp[:, tt, :], axis=AX.X),
                        rd=[brk[tt][2]], wr=[brk[tt][5 + j]])
                    op("vector", lambda e, tt=tt, EQ=EQ: e.tensor_mul(
                        rtmp[:, tt, :], EQ, cmbT[:, tt, :]),
                        rd=[EQb, bcmb[tt]], wr=[brk[tt][2]])
                    op("vector", lambda e, tt=tt, j=j: e.reduce_sum(
                        c12[:, tt, j:j + 1], rtmp[:, tt, :], axis=AX.X),
                        rd=[brk[tt][2]], wr=[brk[tt][7 + j]])
                    op("vector", lambda e, tt=tt, j=j: e.tensor_scalar(
                        rsx[:, tt, 4:5], rsx[:, tt, 0 + j:1 + j], float(CAP),
                        rsx[:, tt, 2 + j:3 + j], ALU.mult, ALU.add),
                        rd=[brk[tt][3 + j], brk[tt][5 + j]], wr=[brk[tt][9]])
                    op("vector", lambda e, tt=tt: e.tensor_scalar_min(
                        rsx[:, tt, 5:6], rsx[:, tt, 4:5], float(NE * CAP - 1)),
                        rd=[brk[tt][9]], wr=[brk[tt][10]])
                    op("vector", lambda e, tt=tt, j=j: e.tensor_copy(
                        aidx[:, tt, j:j + 1], rsx[:, tt, 5:6]),
                        rd=[brk[tt][10]], wr=[brk[tt][14]])
                    op("vector", lambda e, tt=tt, j=j: e.tensor_scalar(
                        rsx[:, tt, 6:7], rsx[:, tt, 2 + j:3 + j], float(CAP), None, ALU.is_lt),
                        rd=[brk[tt][5 + j]], wr=[brk[tt][11]])
                    op("vector", lambda e, tt=tt: e.tensor_scalar_add(
                        rsx[:, tt, 7:8], rsx[:, tt, 4:5], -BIGOFF),
                        rd=[brk[tt][9]], wr=[brk[tt][12]])
                    op("vector", lambda e, tt=tt: e.tensor_mul(
                        rsx[:, tt, 7:8], rsx[:, tt, 6:7], rsx[:, tt, 7:8]),
                        rd=[brk[tt][11]], wr=[brk[tt][12]])
                    op("vector", lambda e, tt=tt: e.tensor_scalar_add(
                        rsx[:, tt, 7:8], rsx[:, tt, 7:8], BIGOFF),
                        wr=[brk[tt][12]])
                    op("vector", lambda e, tt=tt, j=j: e.tensor_copy(
                        oidx[:, tt, j:j + 1], rsx[:, tt, 7:8]),
                        rd=[brk[tt][12]], wr=[brk[tt][13]])


        def load_qkv(hp):
            op("gpsimd", lambda g, hp=hp: g.dma_start(
                qkvS[:, 0, :, :], qkvw[hp].rearrange("(ct p) o -> p ct o", p=128)),
                wr=[bqkv[0]], dma="dQ0")

        def load_prj(hp):
            s8 = hp % 6
            op("gpsimd", lambda g, hp=hp, s8=s8: g.dma_start(
                prjS[:, s8, :], projw[128 * hp:128 * hp + 128, :]),
                wr=[bprj[s8]], dma=f"dP{s8}")

        proj_jobs = []

        def make_proj_block(block_hps):
            with_bias = block_hps[0] == 0
            for tt in range(NTT):
                for oc in range(NCH):
                    kb = 4 + (tt * NCH + oc) % 2
                    ol = min(512, C - 512 * oc)
                    n = len(block_hps)
                    for j, hp2 in enumerate(block_hps):
                        sa = hp2 % 8
                        s6 = hp2 % 6
                        first = j == 0
                        closing = (j == n - 1) and not with_bias
                        proj_jobs.append((
                            "tensor",
                            (lambda e, tt=tt, oc=oc, ol=ol, kb=kb, sa=sa, s6=s6,
                                    first=first, closing=closing: e.matmul(
                                pP[kb][:, 0:ol], ahp[:, sa, 128 * tt:128 * tt + 128],
                                prjS[:, s6, 512 * oc:512 * oc + ol],
                                start=first, stop=closing)),
                            dict(rd=[bahp[b2 % 8] for b2 in block_hps] +
                                    [bprj[b2 % 6] for b2 in block_hps],
                                 wr=[pB[kb]]) if closing else
                            dict(rd_nt=(bahp[sa], bprj[s6]),
                                 war=(pB[kb],) if first else (), inc=False)))
                    if with_bias:
                        proj_jobs.append((
                            "tensor",
                            (lambda e, oc=oc, ol=ol, kb=kb: e.matmul(
                                pP[kb][:, 0:ol], ones1[0:1, :],
                                pbrow[0:1, 512 * oc:512 * oc + ol],
                                start=False, stop=True)),
                            dict(rd=[bahp[b2 % 8] for b2 in block_hps] +
                                    [bprj[b2 % 6] for b2 in block_hps] + [bconst],
                                 wr=[pB[kb]])))
                    proj_jobs.append((
                        "vector",
                        (lambda e, tt=tt, oc=oc, ol=ol, kb=kb: e.tensor_add(
                            accT[:, tt, 512 * oc:512 * oc + ol],
                            accT[:, tt, 512 * oc:512 * oc + ol], pP[kb][:, 0:ol])),
                        dict(rd=[pB[kb], bacc[tt]], wr=[bacc[tt]])))

        def drain_proj(nj):
            while nj and proj_jobs:
                eng, fn, kw = proj_jobs.pop(0)
                op(eng, fn, **kw)
                nj -= 1

        HPL = 0 if cfg.get("no_attn") else HP
        if HPL:
            load_qkv(0)
            load_prj(0)
        chunks_kv = [(0, 512), (512, 512), (1024, GW - 1024)]
        chunks_q = [(own0, 512), (own0 + 512, 512)]
        for hp in range(HPL):
            p = hp % 2
            for mi, (dst, bdst) in enumerate(((Qf, bQ), (Kf, bK), (Vf, bV))):
                chunks = chunks_q if mi == 0 else chunks_kv
                for ci, (c0, cl) in enumerate(chunks):
                    kb = (mi * len(chunks) + ci) % 2
                    for ct in range(CT):
                        last = ct == CT - 1
                        op("tensor", lambda e, ct=ct, mi=mi, c0=c0, cl=cl, kb=kb: e.matmul(
                            pP[kb][:, 0:cl], qkvS[:, 0, ct, 128 * mi:128 * mi + 128],
                            xnT[:, ct, c0:c0 + cl], start=(ct == 0), stop=(ct == CT - 1)),
                            rd=[bxnT[ct], bqkv[0]] if last else (),
                            wr=[pB[kb]] if last else (),
                            rd_nt=() if last else (bxnT[ct], bqkv[0]),
                            war=(pB[kb],) if ct == 0 else (),
                            inc=last)
                    op("vector", lambda e, dst=dst, c0=c0, cl=cl, kb=kb: e.tensor_copy(
                        dst[:, c0:c0 + cl], pP[kb][:, 0:cl]), rd=[pB[kb]], wr=[bdst])
            if hp + 1 < HPL:
                load_qkv(hp + 1)
            NU = 2 * NQT

            def uparams(k):
                qt, hh = divmod(k, 2)
                if grp == 0:
                    w0, mid = (0, 1) if qt == 0 else (128 * qt - 8, 0)
                else:
                    w0, mid = (128 * qt, 0) if qt < NQT - 1 else (128 * qt, 2)
                return qt, hh, w0, mid, own0 + 128 * qt, 64 * hh

            def emit_vw(qt):
                qt_, hh_, w0, mid, q0, d0 = uparams(2 * qt)
                tp = qt % 4
                kbt = 0 + qt % 2
                op("tensor", lambda e, w0=w0, kbt=kbt: e.transpose(
                    pbf(kbt)[:, 0:128], Vf[:, w0:w0 + 128], idb[:, :]),
                    rd=[bV, bconst], war=[pB[kbt]], wr=[pB[kbt]])
                op("tensor", lambda e, w0=w0, kbt=kbt: e.transpose(
                    pbf(kbt)[0:16, 128:256], Vf[:, w0 + 128:w0 + WIN], idb[:, :]),
                    rd=[bV, bconst], wr=[pB[kbt]])
                op("vector", lambda e, kbt=kbt, tp=tp: e.tensor_copy(
                    vwa[:, tp, :], pbf(kbt)[:, 0:128]), rd=[pB[kbt]], wr=[bvwa[tp]])
                op("vector", lambda e, kbt=kbt, tp=tp: e.tensor_copy(
                    vwb[0:16, tp, :], pbf(kbt)[0:16, 128:256]), rd=[pB[kbt]], wr=[bvwb[tp]])

            def emit_scores(k):
                qt, hh, w0, mid, q0, d0 = uparams(k)
                kbs = 2 + k % 2
                op("tensor", lambda e, q0=q0, w0=w0, d0=d0, kbs=kbs: e.matmul(
                    pP[kbs][:, 0:WIN], Qf[d0:d0 + 64, q0:q0 + 128],
                    Kf[d0:d0 + 64, w0:w0 + WIN], start=True, stop=True),
                    rd=[bQ, bK], war=[pB[kbs]], wr=[pB[kbs]])

            def emit_mask(k):
                qt, hh, w0, mid, q0, d0 = uparams(k)
                kbs = 2 + k % 2
                u = k % 4
                u3 = k % 3
                op("vector", lambda e, u3=u3, kbs=kbs, mid=mid: e.tensor_mul(
                    scm[:, u3, :], pP[kbs][:, 0:WIN], mcs[:, mid, :]),
                    rd=[pB[kbs], bconst], wr=[bscm[u3]])
                op("vector", lambda e, u3=u3, mid=mid: e.tensor_add(
                    scm[:, u3, :], scm[:, u3, :], mas[:, mid, :]),
                    rd=[bconst], wr=[bscm[u3]])
                op("vector", lambda e, u=u, u3=u3: e.reduce_max(
                    sst[:, u, 0:1], scm[:, u3, :], axis=AX.X, negate=True),
                    rd=[bscm[u3]], wr=[bsst[u]])

            def emit_exp(k):
                u = k % 4
                u3 = k % 3
                op("scalar", lambda e, u=u, u3=u3: e.activation(
                    exv[:, u, :], scm[:, u3, :], AF.Exp, bias=sst[:, u, 0:1],
                    accum_out=sst[:, u, 1:2]), rd=[bscm[u3], bsst[u]], wr=[bexv[u], bsst[u]])

            def emit_scale(k):
                u = k % 4
                op("vector", lambda e, u=u: e.reciprocal(
                    sst[:, u, 2:3], sst[:, u, 1:2]), rd=[bsst[u]], wr=[bsst[u]])
                op("vector", lambda e, u=u: e.tensor_scalar_mul(
                    abf[:, u, :], exv[:, u, :], sst[:, u, 2:3]),
                    rd=[bsst[u], bexv[u]], wr=[babf[u]])

            def emit_trans(k):
                u = k % 4
                kb2 = 6 + k % 2
                op("tensor", lambda e, u=u, kb2=kb2: e.transpose(
                    pbf(kb2)[:, 0:128], abf[:, u, 0:128], idb[:, :]),
                    rd=[babf[u], bconst], war=[pB[kb2]], wr=[pB[kb2]])
                op("tensor", lambda e, u=u, kb2=kb2: e.transpose(
                    pbf(kb2)[0:16, 128:256], abf[:, u, 128:WIN], idb[:, :]),
                    rd=[babf[u], bconst], wr=[pB[kb2]])
                op("vector", lambda e, u=u, kb2=kb2: e.tensor_copy(
                    aTa[:, u, :], pbf(kb2)[:, 0:128]), rd=[pB[kb2]], wr=[baTa[u]])
                op("vector", lambda e, u=u, kb2=kb2: e.tensor_copy(
                    aTb[0:16, u, :], pbf(kb2)[0:16, 128:256]), rd=[pB[kb2]], wr=[baTb[u]])

            def emit_av(k, p):
                qt, hh, w0, mid, q0, d0 = uparams(k)
                u = k % 4
                kba = 6 + k % 2
                tp = qt % 4
                s8 = hp % 8
                op("tensor", lambda e, u=u, kba=kba, d0=d0, tp=tp: e.matmul(
                    pP[kba][0:64, 256:384], vwa[:, tp, d0:d0 + 64], aTa[:, u, :],
                    start=True, stop=False),
                    rd_nt=(bvwa[tp], baTa[u]), war=(pB[kba],), inc=False)
                op("tensor", lambda e, u=u, kba=kba, d0=d0, tp=tp: e.matmul(
                    pP[kba][0:64, 256:384], vwb[0:16, tp, d0:d0 + 64], aTb[0:16, u, :],
                    start=False, stop=True),
                    rd=[bvwa[tp], bvwb[tp], baTa[u], baTb[u]], wr=[pB[kba]])
                op("vector", lambda e, u=u, kba=kba, hh=hh, qt=qt, s8=s8: e.tensor_copy(
                    ahp[64 * hh:64 * hh + 64, s8, 128 * qt:128 * qt + 128],
                    pP[kba][0:64, 256:384]), rd=[pB[kba]], wr=[bahp[s8]])

            for step in range(NU + 6):
                if 6 <= step <= NU + 5:
                    emit_av(step - 6, p)
                if 5 <= step <= NU + 4:
                    emit_trans(step - 5)
                if 3 <= step <= NU + 2:
                    emit_scale(step - 3)
                if 2 <= step <= NU + 1:
                    emit_exp(step - 2)
                if 1 <= step <= NU:
                    emit_mask(step - 1)
                if step < NU:
                    if step % 2 == 0:
                        emit_vw(step // 2)
                    emit_scores(step)
                drain_proj(4)
            if hp % 4 == 3 or hp == HPL - 1:
                make_proj_block(list(range(4 * (hp // 4), hp + 1)))
            if hp + 1 < HPL:
                load_prj(hp + 1)
        drain_proj(len(proj_jobs))
        emit_routing()

        if cfg.get("no_moe"):
            for tt in range(NTT):
                t0 = grp * GT + 128 * tt
                op("sync", lambda e_, tt=tt, t0=t0: e_.dma_start(
                    y[t0:t0 + 128, :], accT[:, tt, :]), rd=[bacc[tt]], dma=f"dO{tt}")
            b.fence(LNATTN_BUFS, LNATTN_BUFS)
            continue
        if cfg.get("dbg5") and grp == 0:
            op("sync", lambda e_: e_.dma_start(
                gdbg[:, 4608:4608 + 512], xnT[:, 0, 0:512]), rd=[bxnT[0]], dma="dHI")
        b.fence(LNATTN_BUFS, MOE_BUFS)
        QC = C // 4
        for tt in range(NTT):
            kx = tt % 2
            op("sync", lambda e_, tt=tt, kx=kx, own0=own0: e_.dma_start(
                xe[:, kx, :], xtm[own0 + 128 * tt:own0 + 128 * tt + 128, :]),
                rd=[bxtm], wr=[bxe[kx]], dma=f"dXE{kx}")
            for j in range(2):
                op("gpsimd", lambda g, tt=tt, j=j, kx=kx: g.indirect_dma_start(
                    xdram[:, :], bass.IndirectOffsetOnAxis(oidx[:, tt, j:j + 1], 0),
                    xe[:, kx, :], None,
                    bounds_check=NE * CAP - 1, oob_is_err=False),
                    rd=[bxe[kx], brk[tt][13]], wr=[bxdram], dma="dSC")
        if cfg.get("dbg5") and grp == 0:
            op("sync", lambda e_: e_.dma_start(
                gdbg[:, 5120:5120 + 512], xnT[:, 0, 0:512]), rd=[bxnT[0]], dma="dHJ")
        especs = [("e", e, None) for e in range(NE)] + \
                 [("s", sh, th) for sh in range(2) for th in range(2)]
        if cfg.get("dbg1"):
            especs = especs[:NE]

        def gu_dma(spec, it, pair):
            kind, e, th = spec
            if kind == "e":
                srcg = gw[e][:, 128 * it:128 * it + 128]
                srcu = uwt[e][:, 128 * it:128 * it + 128]
            else:
                srcg = sgw[:, SIH * e + 128 * it:SIH * e + 128 * it + 128]
                srcu = suw[:, SIH * e + 128 * it:SIH * e + 128 * it + 128]
            s0 = (2 * pair) % NGU
            s1 = (2 * pair + 1) % NGU
            op("gpsimd", lambda g, srcg=srcg, s0=s0: g.dma_start(
                guS[:, s0, :, :], srcg.rearrange("(ct p) i -> p ct i", p=128)),
                wr=[bgu[s0]], dma=f"dG{s0}")
            op("gpsimd", lambda g, srcu=srcu, s1=s1: g.dma_start(
                guS[:, s1, :, :], srcu.rearrange("(ct p) i -> p ct i", p=128)),
                wr=[bgu[s1]], dma=f"dG{s1}")

        dwctr = [0]

        def dw_dma(spec, it):
            kind, e, th = spec
            if kind == "e":
                srcd = dwt[e][128 * it:128 * it + 128, :]
            else:
                srcd = sdw[SIH * e + 128 * it:SIH * e + 128 * it + 128, :]
            sl = dwctr[0] % NDW
            dwctr[0] += 1
            op("gpsimd", lambda g, srcd=srcd, sl=sl: g.dma_start(
                dwS[:, sl, :], srcd), wr=[bdw[sl]], dma=f"dD{sl}")
            return sl

        gu_q = [(spec, it) for spec in especs for it in range((IT if spec[0] == "e" else SIT))]
        gu_next = 0
        for _ in range(3):
            if gu_next < len(gu_q):
                gu_dma(gu_q[gu_next][0], gu_q[gu_next][1], gu_next)
                gu_next += 1

        for si, spec in enumerate(especs):
            kind, e, th = spec
            nit = IT if kind == "e" else SIT
            if kind == "e":
                for k in range(4):
                    kx = k % 2
                    op("sync", lambda e_, e=e, k=k, kx=kx: e_.dma_start(
                        xe[:, kx, :], xdram[e * CAP + 128 * k:e * CAP + 128 * k + 128, :]),
                        rd=[bxdram], wr=[bxe[kx]], dma=f"dXE{kx}")
                    for ct in range(CT):
                        kb = 4 + ct % 2
                        op("tensor", lambda e_, ct=ct, kx=kx, kb=kb: e_.transpose(
                            pbf(kb)[:, 0:128], xe[:, kx, 128 * ct:128 * ct + 128], idb[:, :]),
                            rd=[bxe[kx], bconst], war=[pB[kb]], wr=[pB[kb]])
                        op("vector", lambda e_, ct=ct, k=k, kb=kb: e_.tensor_copy(
                            xeT[:, ct, 128 * k:128 * k + 128], pbf(kb)[:, 0:128]),
                            rd=[pB[kb]], wr=[bxeT])
            if cfg.get("dbg5") and grp == 0 and spec == ("s", 0, 0):
                op("sync", lambda e_: e_.dma_start(
                    gdbg[:, 0:CT * 128], guS[:, 0, :, :].rearrange("p ct i -> p (ct i)")),
                    rd=[bgu[0]], dma="dHE")
                op("sync", lambda e_: e_.dma_start(
                    gdbg[:, 1024:1024 + CT * 128], guS[:, 1, :, :].rearrange("p ct i -> p (ct i)")),
                    rd=[bgu[1]], dma="dHF")
                for ct in range(CT):
                    op("sync", lambda e_, ct=ct, own0=own0: e_.dma_start(
                        gdbg[:, 2048 + ct * 512:2048 + ct * 512 + 512],
                        xnT[:, ct, own0:own0 + 512]), rd=[bxnT[ct]], dma=f"dHG{ct}")
            if cfg.get("dbg5") and grp == 0 and kind == "e" and e in (0, 3, 7):
                c0d = 5632 + 512 * (0 if e == 0 else (1 if e == 3 else 2))
                op("sync", lambda e_, c0d=c0d: e_.dma_start(
                    gdbg[:, c0d:c0d + 512], xnT[:, 0, 0:512]), rd=[bxnT[0]], dma=f"dHK{e}")
            for it in range(nit):
                pair = (sum((IT if s[0] == "e" else SIT) for s in especs[:si]) + it)
                s0 = (2 * pair) % NGU
                s1 = (2 * pair + 1) % NGU
                kbg = (it % 2) * 2
                kbu = kbg + 1
                for ct in range(CT):
                    last = ct == CT - 1
                    mv_ap = (lambda e_, ct=ct, s0=s0, kbg=kbg, kind=kind, th=th, own0=own0: e_.matmul(
                        pP[kbg][:, :],
                        guS[:, s0, ct, :],
                        xeT[:, ct, :] if kind == "e" else xnT[:, ct, own0 + 512 * th:own0 + 512 * th + 512],
                        start=(ct == 0), stop=(ct == CT - 1)))
                    mvb = bxeT if kind == "e" else bxnT[ct]
                    op("tensor", mv_ap,
                        rd=[bgu[s0], mvb] if last else (),
                        wr=[pB[kbg]] if last else (),
                        rd_nt=() if last else (bgu[s0], mvb),
                        war=(pB[kbg],) if ct == 0 else (),
                        inc=last)
                for ct in range(CT):
                    last = ct == CT - 1
                    mv_ap = (lambda e_, ct=ct, s1=s1, kbu=kbu, kind=kind, th=th, own0=own0: e_.matmul(
                        pP[kbu][:, :],
                        guS[:, s1, ct, :],
                        xeT[:, ct, :] if kind == "e" else xnT[:, ct, own0 + 512 * th:own0 + 512 * th + 512],
                        start=(ct == 0), stop=(ct == CT - 1)))
                    mvb = bxeT if kind == "e" else bxnT[ct]
                    op("tensor", mv_ap,
                        rd=[bgu[s1], mvb] if last else (),
                        wr=[pB[kbu]] if last else (),
                        rd_nt=() if last else (bgu[s1], mvb),
                        war=(pB[kbu],) if ct == 0 else (),
                        inc=last)
                sp = it % 2
                if kind == "e":
                    op("scalar", lambda e_, sp=sp, kbg=kbg: e_.activation(
                        sgS[:, sp, :], pP[kbg][:, :], AF.Silu), rd=[pB[kbg]], wr=[bsg[sp]])
                    op("vector", lambda e_, it=it, sp=sp, kbu=kbu: e_.tensor_mul(
                        hS[:, it, :], sgS[:, sp, :], pP[kbu][:, :]),
                        rd=[bsg[sp], pB[kbu]], wr=[bh[it]])
                else:
                    bidx = SIT * e + it
                    op("scalar", lambda e_, sp=sp, kbg=kbg, bidx=bidx: e_.activation(
                        sgS[:, sp, :], pP[kbg][:, :], AF.Silu, bias=sgbC[:, bidx:bidx + 1]),
                        rd=[pB[kbg], bconst], wr=[bsg[sp]])
                    op("vector", lambda e_, it=it, sp=sp, kbu=kbu, bidx=bidx: e_.scalar_tensor_tensor(
                        hS[:, it, :], pP[kbu][:, :],
                        subC[:, bidx:bidx + 1], sgS[:, sp, :], ALU.add, ALU.mult),
                        rd=[bsg[sp], pB[kbu], bconst], wr=[bh[it]])
                if gu_next < len(gu_q):
                    gu_dma(gu_q[gu_next][0], gu_q[gu_next][1], gu_next)
                    gu_next += 1
            if cfg.get("dbg5") and grp == 0 and spec == ("s", 0, 0):
                op("sync", lambda e_: e_.dma_start(
                    hdbg[:, 0:SIT * 512], hS[:, 0:SIT, :]), rd=[bh[i] for i in range(SIT)], dma="dHD")
            if cfg.get("dbg5") and grp == 0 and spec == ("e", 7, None):
                op("sync", lambda e_: e_.dma_start(
                    gdbg[:, 7168:7168 + 512], xnT[:, 0, 0:512]), rd=[bxnT[0]], dma="dHL")
            bias_close = (kind == "s" and e == 0)
            for och in range(2):
                slq = [dw_dma(spec, i) for i in range(min(nit, 4))]
                for it in range(nit):
                    if it + 4 < nit:
                        slq.append(dw_dma(spec, it + 4))
                    sl = slq[it]
                    lastit = it == nit - 1
                    for k in range(4):
                        for oc in range(2):
                            col = (och * 2 + oc) * QC
                            last = lastit and not bias_close
                            trk = (k == 3 and oc == 1) or last
                            op("tensor", lambda e_, it=it, k=k, oc=oc, col=col, sl=sl, last=last: e_.matmul(
                                pP[2 * k + oc][:, 0:QC], hS[:, it, 128 * k:128 * k + 128],
                                dwS[:, sl, col:col + QC],
                                start=(it == 0), stop=last),
                                rd=[bh[it], bdw[sl]] if trk else (),
                                wr=[pB[2 * k + oc]] if last else (),
                                rd_nt=() if trk else (bh[it], bdw[sl]),
                                war=(pB[2 * k + oc],) if it == 0 else (),
                                inc=trk)
                if bias_close:
                    for k in range(4):
                        for oc in range(2):
                            col = (och * 2 + oc) * QC
                            op("tensor", lambda e_, k=k, oc=oc, col=col: e_.matmul(
                                pP[2 * k + oc][:, 0:QC], ones1[32:33, :],
                                sdbrow[0:1, col:col + QC], start=False, stop=True),
                                rd=[bconst] + [bh[i] for i in range(nit)],
                                wr=[pB[2 * k + oc]])
                for k in range(4):
                    if kind == "e":
                        kx = k % 2
                        for oc in range(2):
                            col = (och * 2 + oc) * QC
                            op("vector", lambda e_, k=k, oc=oc, col=col, kx=kx: e_.tensor_copy(
                                stg[:, kx, col:col + QC], pP[2 * k + oc][:, 0:QC]),
                                rd=[pB[2 * k + oc]], wr=[bstg[kx]])
                        op("sync", lambda e_, e=e, k=k, och=och, kx=kx: e_.dma_start(
                            ddram[e * CAP + 128 * k:e * CAP + 128 * k + 128,
                                  och * 2 * QC:och * 2 * QC + 2 * QC],
                            stg[:, kx, och * 2 * QC:och * 2 * QC + 2 * QC]),
                            rd=[bstg[kx]], wr=[bddram], dma=f"dDD{kx}")
                    else:
                        tt = th * 4 + k
                        for oc in range(2):
                            col = (och * 2 + oc) * QC
                            op("vector", lambda e_, tt=tt, k=k, oc=oc, col=col: e_.tensor_add(
                                accT[:, tt, col:col + QC],
                                accT[:, tt, col:col + QC], pP[2 * k + oc][:, 0:QC]),
                                rd=[pB[2 * k + oc], bacc[tt]], wr=[bacc[tt]])
            if cfg.get("dbg5") and grp == 0 and spec == ("e", 7, None):
                op("sync", lambda e_: e_.dma_start(
                    gdbg[:, 7680:7680 + 512], xnT[:, 0, 0:512]), rd=[bxnT[0]], dma="dHM")
            if kind == "e" and e == NE - 1 and not cfg.get("dbg2"):
                for tt in range(NTT):
                    for j in range(2):
                        kx = j % 2
                        op("gpsimd", lambda g, tt=tt, j=j, kx=kx: g.indirect_dma_start(
                            xe[:, kx, :], None, ddram[:, :],
                            bass.IndirectOffsetOnAxis(aidx[:, tt, j:j + 1], 0)),
                            rd=[bddram, brk[tt][14]], wr=[bxe[kx]], dma=f"dGA{kx}")
                        op("vector", lambda e_, tt=tt, j=j, kx=kx: e_.scalar_tensor_tensor(
                            accT[:, tt, :], xe[:, kx, :], c12[:, tt, j:j + 1],
                            accT[:, tt, :], ALU.mult, ALU.add),
                            rd=[bxe[kx], brk[tt][7 + j], bacc[tt]], wr=[bacc[tt]])
            if cfg.get("dbg5") and grp == 0 and spec == ("e", 7, None):
                op("sync", lambda e_: e_.dma_start(
                    gdbg[:, 8192:8192 + 512], xnT[:, 0, 0:512]), rd=[bxnT[0]], dma="dHN")
            if (kind == "s" and e == 1 and th == 1) or (cfg.get("dbg1") and si == len(especs) - 1):
                for tt in range(NTT):
                    t0 = grp * GT + 128 * tt
                    op("sync", lambda e_, tt=tt, t0=t0: e_.dma_start(
                        y[t0:t0 + 128, :], accT[:, tt, :]), rd=[bacc[tt]], dma=f"dO{tt}")

    fin = []
    for tt in range(NTT):
        name = f"dO{tt}"
        if name in b.sems:
            fin.append((b.sems[name], b.cnt[name]))
    b.plan["sync"].append((lambda e: e.nop(), fin, None, False))
    b.build_block()
    return b


def prep_inputs(L0_attn_x, L0_attn_qkv_w, L0_attn_proj_w, L0_attn_proj_b,
                L0_ffn_router_w, L0_ffn_gate_w, L0_ffn_up_w, L0_ffn_down_w,
                L0_ffn_s_gate_w, L0_ffn_s_gate_b, L0_ffn_s_up_w, L0_ffn_s_up_b,
                L0_ffn_s_down_w, L0_ffn_s_down_b, cfg):
    C, HP = cfg["C"], cfg["HEADS"] // 2
    qkv = np.asarray(L0_attn_qkv_w, np.float32)
    qkvw = np.empty((HP, C, 384), np.float32)
    for hp in range(HP):
        blk = np.concatenate([qkv[128 * hp:128 * hp + 128],
                              qkv[C + 128 * hp:C + 128 * hp + 128],
                              qkv[2 * C + 128 * hp:2 * C + 128 * hp + 128]], 0)
        qkvw[hp] = np.ascontiguousarray(blk.T)
    mc, ma = make_masks(C // cfg["HEADS"])
    shared = {
        "qkvw": qkvw,
        "projw": np.ascontiguousarray(np.asarray(L0_attn_proj_w, np.float32).T),
        "rw": np.ascontiguousarray(np.asarray(L0_ffn_router_w, np.float32).T),
        "gw": np.ascontiguousarray(np.asarray(L0_ffn_gate_w, np.float32)),
        "uw": np.ascontiguousarray(np.asarray(L0_ffn_up_w, np.float32)),
        "dw": np.ascontiguousarray(np.asarray(L0_ffn_down_w, np.float32)),
        "sgw": np.ascontiguousarray(np.asarray(L0_ffn_s_gate_w, np.float32)),
        "suw": np.ascontiguousarray(np.asarray(L0_ffn_s_up_w, np.float32)),
        "sdw": np.ascontiguousarray(np.asarray(L0_ffn_s_down_w, np.float32)),
        "pb": np.ascontiguousarray(np.asarray(L0_attn_proj_b, np.float32)),
        "sgb": np.ascontiguousarray(np.asarray(L0_ffn_s_gate_b, np.float32)),
        "sub": np.ascontiguousarray(np.asarray(L0_ffn_s_up_b, np.float32)),
        "sdb": np.ascontiguousarray(np.asarray(L0_ffn_s_down_b, np.float32)),
        "ident": np.eye(128, dtype=np.float32),
        "ltri": np.tril(np.ones((128, 128), np.float32), -1).T,
        "iota8": np.broadcast_to(np.arange(cfg["NE"], dtype=np.float32), (128, cfg["NE"])).copy(),
        "mc": mc,
        "ma": ma,
    }
    xs = np.ascontiguousarray(np.asarray(L0_attn_x, np.float32))
    return xs, shared


def kernel(**inputs):
    cfg = FULL_CFG
    xs, shared = prep_inputs(cfg=cfg, **inputs)
    Bn = xs.shape[0]
    bld = build_program(cfg)
    in_maps = [dict(shared, x=xs[bb]) for bb in range(Bn)]
    try:
        res = run_bass_kernel_spmd(bld.nc, in_maps, core_ids=list(range(Bn)))
    finally:
        bld.stack.close()
    return np.stack([np.asarray(r["y"], np.float32) for r in res.results], 0)


# version 16
# speedup vs baseline: 1.1469x; 1.1469x over previous
import math
from contextlib import ExitStack

import numpy as np

import concourse.bass as bass
import concourse.mybir as mybir
from concourse.bass_utils import run_bass_kernel_spmd

F32 = mybir.dt.float32
BF16 = mybir.dt.bfloat16
I32 = mybir.dt.int32
AF = mybir.ActivationFunctionType
ALU = mybir.AluOpType
AX = mybir.AxisListType

ENGS = ("tensor", "vector", "scalar", "gpsimd", "sync")
ESEM = {"tensor": "eT", "vector": "eV", "scalar": "eS", "gpsimd": "eG", "sync": "eY"}

FULL_CFG = dict(C=2048, HEADS=32, NE=8, I=1408, SIH=1408)
N_TOK = 2048
GT = 1024
REAL = 1032
GW = 1040
WIN = 144
NQT = GT // 128
NTT = GT // 128
NLT = (REAL + 127) // 128


def make_masks(hd):
    scale = hd ** -0.5
    mc = np.zeros((3, 128, WIN), np.float32)
    ma = np.full((3, 128, WIN), -1e30, np.float32)
    for p in range(128):
        for jp in range(p, p + 16):
            d = abs(p + 8 - jp)
            mc[0, p, jp] = scale * (1 - d / 16.0)
            ma[0, p, jp] = 0.0
        for jp in range(max(0, p - 8), p + 8):
            d = abs(p - jp)
            mc[1, p, jp] = scale * (1 - d / 16.0)
            ma[1, p, jp] = 0.0
        for jp in range(p, min(p + 16, 136)):
            d = abs(p + 8 - jp)
            mc[2, p, jp] = scale * (1 - d / 16.0)
            ma[2, p, jp] = 0.0
    return mc, ma


class Buf:
    __slots__ = ("w", "r")

    def __init__(self):
        self.w = None
        self.r = []


class Bld:
    def __init__(self):
        self.nc = bass.Bass("TRN2", target_bir_lowering=False)
        self.stack = ExitStack()
        self.plan = {e: [] for e in ENGS}
        self.sems = {}
        self.cnt = {}
        self.wm = {}

    def sem(self, name):
        if name not in self.sems:
            self.sems[name] = self.stack.enter_context(self.nc.semaphore(name))
            self.cnt[name] = 0
        return self.sems[name]

    def op(self, eng, fn, rd=(), wr=(), war=(), rd_nt=(), inc=True, dma=None):
        waits = {}

        def need(tok):
            if tok is not None and tok[1] > waits.get(tok[0], 0):
                waits[tok[0]] = tok[1]

        for b in rd:
            need(b.w)
        for b in rd_nt:
            need(b.w)
        for b in war:
            need(b.w)
            for t in b.r:
                need(t)
        for b in wr:
            need(b.w)
            for t in b.r:
                need(t)
        real = []
        for s, v in waits.items():
            k = (eng, s)
            if dma is not None or self.wm.get(k, 0) < v:
                real.append((self.sem(s), v))
                if self.wm.get(k, 0) < v:
                    self.wm[k] = v
        tok = None
        if dma is not None:
            self.sem(dma)
            self.cnt[dma] += 16
            tok = (dma, self.cnt[dma])
        elif inc:
            s = ESEM[eng]
            self.sem(s)
            self.cnt[s] += 1
            tok = (s, self.cnt[s])
        self.plan[eng].append((fn, real, self.sems[tok[0]] if tok else None, bool(dma)))
        if tok is not None:
            for b in wr:
                b.w = tok
                b.r = []
            for b in rd:
                b.r.append(tok)
        else:
            assert not wr and not rd
        return tok

    def fence(self, outgoing, incoming):
        fl = [(s, v) for s, v in self.cnt.items() if v > 0]
        for b in incoming:
            b.r = b.r + [t for t in fl if t != b.w]

    def build_block(self):
        with self.nc.Block() as block:
            def run(eng_obj, lst):
                for fn, waits, incsem, isdma in lst:
                    for s, v in waits:
                        eng_obj.wait_ge(s, v)
                    ins = fn(eng_obj)
                    if incsem is not None:
                        ins.then_inc(incsem, 16 if isdma else 1)
            block.tensor(lambda e: run(e, self.plan["tensor"]))
            block.vector(lambda e: run(e, self.plan["vector"]))
            block.scalar(lambda e: run(e, self.plan["scalar"]))
            block.gpsimd(lambda e: run(e, self.plan["gpsimd"]))
            block.sync(lambda e: run(e, self.plan["sync"]))


def build_program(cfg):
    C, HEADS, NE, I, SIH = cfg["C"], cfg["HEADS"], cfg["NE"], cfg["I"], cfg["SIH"]
    HD = C // HEADS
    assert HD == 64
    CT = C // 128
    HP = HEADS // 2
    IT = I // 128
    SIT = SIH // 128
    NCH = math.ceil(C / 512)
    NBN = math.ceil(C / 512)
    b = Bld()
    nc = b.nc
    op = b.op

    x = nc.dram_tensor("x", [N_TOK, C], F32, kind="ExternalInput").ap()
    qkvw = nc.dram_tensor("qkvw", [HP, C, 384], F32, kind="ExternalInput").ap()
    projw = nc.dram_tensor("projw", [C, C], F32, kind="ExternalInput").ap()
    rw = nc.dram_tensor("rw", [C, NE], F32, kind="ExternalInput").ap()
    gw = nc.dram_tensor("gw", [NE, C, I], F32, kind="ExternalInput").ap()
    uwt = nc.dram_tensor("uw", [NE, C, I], F32, kind="ExternalInput").ap()
    dwt = nc.dram_tensor("dw", [NE, I, C], F32, kind="ExternalInput").ap()
    sgw = nc.dram_tensor("sgw", [C, 2 * SIH], F32, kind="ExternalInput").ap()
    suw = nc.dram_tensor("suw", [C, 2 * SIH], F32, kind="ExternalInput").ap()
    sdw = nc.dram_tensor("sdw", [2 * SIH, C], F32, kind="ExternalInput").ap()
    pbd = nc.dram_tensor("pb", [C], F32, kind="ExternalInput").ap()
    sgbd = nc.dram_tensor("sgb", [2 * SIH], F32, kind="ExternalInput").ap()
    subd = nc.dram_tensor("sub", [2 * SIH], F32, kind="ExternalInput").ap()
    sdbd = nc.dram_tensor("sdb", [C], F32, kind="ExternalInput").ap()
    mcd = nc.dram_tensor("mc", [3, 128, WIN], F32, kind="ExternalInput").ap()
    mad = nc.dram_tensor("ma", [3, 128, WIN], F32, kind="ExternalInput").ap()
    idd = nc.dram_tensor("ident", [128, 128], F32, kind="ExternalInput").ap()
    ltrid = nc.dram_tensor("ltri", [128, 128], F32, kind="ExternalInput").ap()
    iotad = nc.dram_tensor("iota8", [128, NE], F32, kind="ExternalInput").ap()
    y = nc.dram_tensor("y", [N_TOK, C], F32, kind="ExternalOutput").ap()
    hdbg = nc.dram_tensor("hdbg", [128, 4096], BF16, kind="ExternalOutput").ap() if cfg.get("dbg5") else None
    gdbg = nc.dram_tensor("gdbg", [128, 12288], BF16, kind="ExternalOutput").ap() if cfg.get("dbg5") else None
    CAP = 512
    xtm = nc.dram_tensor("xtm", [REAL, C], BF16).ap()
    dwb = nc.dram_tensor("dwb", [NE * I, C], BF16).ap()
    xdram = nc.dram_tensor("xdram", [NE * CAP, C], BF16).ap()
    ddram = nc.dram_tensor("ddram", [NE * CAP, C], BF16).ap()

    A = nc.alloc_sbuf_tensor
    xnT = A("xnT", [128, CT, GW], BF16)
    accT = A("accT", [128, NTT, C], F32)
    mcs = A("mcs", [128, 3, WIN], F32)
    mas = A("mas", [128, 3, WIN], F32)
    idb = A("idb", [128, 128], BF16)
    idf = A("idf", [128, 128], F32)
    ones1 = A("ones1", [33, 128], BF16)
    brow = A("brow", [33, C], BF16)
    pbrow = brow[0:1, :]
    sdbrow = brow[32:33, :]
    sgbC = A("sgbC", [128, 2 * SIT], F32)
    subC = A("subC", [128, 2 * SIT], F32)
    rwS = A("rwS", [128, CT, NE], BF16)
    cmbT = A("cmbT", [128, NTT, NE], F32)
    rsc = A("rsc", [128, NTT, 8 * NE + 4], F32)
    epsb = A("epsb", [128, 1], F32)
    ltriS = A("ltriS", [128, 128], BF16)
    alloneS = A("alloneS", [128, 128], BF16)
    iotaS = A("iotaS", [128, NE], F32)
    indb = A("indb", [128, NTT, NE], BF16)
    rank8 = A("rank8", [128, NTT, NE], F32)
    rtmp = A("rtmp", [128, NTT, NE], F32)
    rsx = A("rsx", [128, NTT, 8], F32)
    oidx = A("oidx", [128, NTT, 2], I32)
    aidx = A("aidx", [128, NTT, 2], I32)
    c12 = A("c12", [128, NTT, 2], F32)
    bno = A("bno", [128, 2, 6 * NBN], F32)
    mv = A("mv", [128, 2, 2], F32)
    lns = A("lns", [128, 2, 4], F32)

    NGU = 6
    LN_SPECS = [
        ("xt_", [128, 2, C], F32), ("xns", [128, 2, C], BF16),
        ("qkvS", [128, 1, CT, 384], BF16), ("prjS", [128, 6, C], BF16),
        ("Qf", [128, GW], BF16), ("Kf", [128, GW], BF16), ("Vf", [128, GW], BF16),
        ("ahp", [128, 8, GT], BF16), ("scm", [128, 3, WIN], F32),
        ("exv", [128, 4, WIN], F32), ("sst", [128, 4, 4], F32),
        ("abf", [128, 4, WIN], BF16),
        ("aTa", [128, 4, 128], BF16), ("aTb", [128, 4, 128], BF16),
        ("vwa", [128, 4, 128], BF16), ("vwb", [128, 4, 128], BF16),
    ]
    NDW = 5
    MOE_SPECS = [
        ("guS", [128, NGU, CT, 128], BF16), ("dwS", [128, NDW, C], BF16),
        ("hS", [128, max(IT, SIT), 512], BF16), ("sgS", [128, 2, 512], BF16),
        ("xe", [128, 2, C], BF16), ("xeT", [128, CT, 512], BF16),
        ("stg", [128, 2, C], BF16),
    ]

    def _sz(specs):
        tot = 0
        for _, shape, dt in specs:
            n = 1
            for s in shape[1:]:
                n *= s
            tot += n if dt == F32 else (n + 1) // 2
        return tot

    ARE = max(_sz(LN_SPECS), _sz(MOE_SPECS)) + 16
    arena = A("arena", [128, ARE], F32)
    ar_f32 = arena.ap()
    ar_bf = arena.ap().bitcast(BF16)

    def carve(specs):
        out = {}
        off = 0
        for name, shape, dt in specs:
            n = 1
            for s in shape[1:]:
                n *= s
            if dt == F32:
                apv = ar_f32[:, off:off + n]
                off += n
            else:
                apv = ar_bf[:, 2 * off:2 * off + n]
                off += (n + 1) // 2
            if len(shape) == 3:
                apv = apv.rearrange("p (a b) -> p a b", a=shape[1])
            elif len(shape) == 4:
                apv = apv.rearrange("p (a b c) -> p a b c", a=shape[1], b=shape[2])
            out[name] = apv
        return out

    V1 = carve(LN_SPECS)
    V2 = carve(MOE_SPECS)
    xt_, xns, qkvS, prjS = V1["xt_"], V1["xns"], V1["qkvS"], V1["prjS"]
    Qf, Kf, Vf, ahp = V1["Qf"], V1["Kf"], V1["Vf"], V1["ahp"]
    scm, exv, sst = V1["scm"], V1["exv"], V1["sst"]
    aTa, aTb, vwa, vwb = V1["aTa"], V1["aTb"], V1["vwa"], V1["vwb"]
    abf = V1["abf"]
    guS, dwS, hS, sgS = V2["guS"], V2["dwS"], V2["hS"], V2["sgS"]
    xe, xeT, stg = V2["xe"], V2["xeT"], V2["stg"]

    pP = [nc.alloc_psum_tensor(f"pP{k}", [128, 512], F32) for k in range(8)]
    pB = [Buf() for _ in range(8)]

    def pbf(k):
        return pP[k].bitcast(BF16)

    bxnT = [Buf() for _ in range(CT)]
    bacc = [Buf() for _ in range(NTT)]
    bconst = Buf()
    bcmb = [Buf() for _ in range(NTT)]
    brr = [[Buf() for _ in range(12)] for _ in range(NTT)]
    bxt = [Buf() for _ in range(2)]
    bxns = [Buf() for _ in range(2)]
    bbno = [Buf() for _ in range(2)]
    bmv = [Buf() for _ in range(2)]
    blns = [Buf() for _ in range(2)]
    bqkv = [Buf() for _ in range(1)]
    bprj = [Buf() for _ in range(6)]
    bQ, bK, bV = Buf(), Buf(), Buf()
    baTa = [Buf() for _ in range(4)]
    baTb = [Buf() for _ in range(4)]
    bvwa = [Buf() for _ in range(4)]
    bvwb = [Buf() for _ in range(4)]
    bscm = [Buf() for _ in range(3)]
    babf = [Buf() for _ in range(4)]
    bexv = [Buf() for _ in range(4)]
    bsst = [Buf() for _ in range(4)]
    bahp = [Buf() for _ in range(8)]
    bgu = [Buf() for _ in range(NGU)]
    bdw = [Buf() for _ in range(NDW)]
    bh = [Buf() for _ in range(max(IT, SIT))]
    bsg = [Buf() for _ in range(2)]
    bxe = [Buf() for _ in range(2)]
    bxeT = Buf()
    bstg = [Buf() for _ in range(2)]
    bxtm = Buf()
    bxdram = Buf()
    bddram = Buf()
    brk = [[Buf() for _ in range(16)] for _ in range(NTT)]

    LNATTN_BUFS = [bQ, bK, bV] + bxt + bxns + bqkv + bprj + baTa + baTb + bvwa + \
        bvwb + bscm + bexv + bsst + bahp + bbno + bmv + blns
    MOE_BUFS = bgu + bdw + bh + bsg + bxe + [bxeT] + bstg

    op("gpsimd", lambda g: g.dma_start(mcs[:, :, :], mcd.rearrange("m p w -> p m w")), wr=[bconst], dma="dC")
    op("gpsimd", lambda g: g.dma_start(mas[:, :, :], mad.rearrange("m p w -> p m w")), wr=[bconst], dma="dC")
    op("gpsimd", lambda g: g.dma_start(idb[:, :], idd[:, :]), wr=[bconst], dma="dC")
    op("gpsimd", lambda g: g.dma_start(idf[:, :], idd[:, :]), wr=[bconst], dma="dC")
    op("gpsimd", lambda g: g.dma_start(ltriS[:, :], ltrid[:, :]), wr=[bconst], dma="dC")
    op("gpsimd", lambda g: g.dma_start(iotaS[:, :], iotad[:, :]), wr=[bconst], dma="dC")
    op("gpsimd", lambda g: g.memset(alloneS[:, :], 1.0), wr=[bconst])
    op("gpsimd", lambda g: g.dma_start(pbrow[0:1, :], pbd[None, :]), wr=[bconst], dma="dC")
    op("gpsimd", lambda g: g.dma_start(sdbrow[0:1, :], sdbd[None, :]), wr=[bconst], dma="dC")
    op("gpsimd", lambda g: g.dma_start(sgbC[:, :], sgbd.rearrange("(t p) -> p t", p=128),
                                       allow_slow_non_contiguous=True), wr=[bconst], dma="dC")
    op("gpsimd", lambda g: g.dma_start(subC[:, :], subd.rearrange("(t p) -> p t", p=128),
                                       allow_slow_non_contiguous=True), wr=[bconst], dma="dC")
    op("gpsimd", lambda g: g.dma_start(rwS[:, :, :], rw.rearrange("(ct p) e -> p ct e", p=128)),
       wr=[bconst], dma="dC")
    op("gpsimd", lambda g: g.memset(epsb[:, :], 1e-5), wr=[bconst])
    op("gpsimd", lambda g: g.memset(ones1[:, :], 1.0), wr=[bconst])

    op("vector", lambda e: e.memset(xe[:, 0, :], 0.0), wr=[bxe[0]])
    for r0 in range(0, NE * CAP, 128):
        op("sync", lambda e, r0=r0: e.dma_start(
            xdram[r0:r0 + 128, :], xe[:, 0, :]), rd=[bxe[0]], wr=[bxdram], dma="dZ")
    b.fence([bxe[0], bxdram], LNATTN_BUFS)

    for grp in range(1 if cfg.get("dbg3") else 2):
        s_g = 0 if grp == 0 else GT - 8
        own0 = 0 if grp == 0 else 8

        if grp > 0:
            b.fence(MOE_BUFS, LNATTN_BUFS)

        def ln_stats(lt):
            rows = 128 if lt < NLT - 1 else REAL - 128 * (NLT - 1)
            t0 = s_g + 128 * lt
            p = lt % 2
            op("sync", lambda e, p=p, t0=t0, rows=rows: e.dma_start(
                xt_[0:rows, p, :], x[t0:t0 + rows, :]), wr=[bxt[p]], dma=f"dX{p}")
            for k in range(NBN):
                cl = min(512, C - 512 * k)
                op("vector", lambda e, p=p, rows=rows, k=k, cl=cl: e.bn_stats(
                    bno[0:rows, p, 6 * k:6 * k + 6], xt_[0:rows, p, 512 * k:512 * k + cl]),
                    rd=[bxt[p]], wr=[bbno[p]])
            op("vector", lambda e, p=p, rows=rows: e.bn_aggr(
                mv[0:rows, p, :], bno[0:rows, p, :]), rd=[bbno[p]], wr=[bmv[p]])
            op("scalar", lambda e, p=p, rows=rows: e.activation(
                lns[0:rows, p, 0:1], mv[0:rows, p, 1:2], AF.Sqrt, bias=epsb[0:rows, 0:1]),
                rd=[bmv[p], bconst], wr=[blns[p]])
            op("vector", lambda e, p=p, rows=rows: e.reciprocal(
                lns[0:rows, p, 1:2], lns[0:rows, p, 0:1]), rd=[blns[p]], wr=[blns[p]])
            op("vector", lambda e, p=p, rows=rows: e.tensor_scalar(
                lns[0:rows, p, 2:3], mv[0:rows, p, 0:1], lns[0:rows, p, 1:2], -1.0,
                ALU.mult, ALU.mult), rd=[bmv[p]], wr=[blns[p]])
            op("scalar", lambda e, p=p, rows=rows: e.activation(
                xns[0:rows, p, :], xt_[0:rows, p, :], AF.Identity,
                bias=lns[0:rows, p, 2:3], scale=lns[0:rows, p, 1:2]),
                rd=[blns[p], bxt[p]], wr=[bxns[p]])
            op("sync", lambda e, p=p, lt=lt, rows=rows: e.dma_start(
                xtm[128 * lt:128 * lt + rows, :], xns[0:rows, p, :]),
                rd=[bxns[p]], wr=[bxtm], dma="dXT")

        def ln_trans(lt):
            rows = 128 if lt < NLT - 1 else REAL - 128 * (NLT - 1)
            p = lt % 2
            col0 = 128 * lt
            for ct in range(CT):
                kb = 6 + (ct % 2)
                op("tensor", lambda e, p=p, ct=ct, rows=rows, kb=kb: e.transpose(
                    pbf(kb)[0:128, 0:rows], xns[0:rows, p, 128 * ct:128 * ct + 128],
                    idb[0:rows, 0:rows]),
                    rd=[bxns[p], bconst], war=[pB[kb]], wr=[pB[kb]])
                op("vector", lambda e, ct=ct, rows=rows, kb=kb, col0=col0: e.tensor_copy(
                    xnT[:, ct, col0:col0 + rows], pbf(kb)[0:128, 0:rows]),
                    rd=[pB[kb]], wr=[bxnT[ct]])

        for tt in range(NTT):
            t0 = grp * GT + 128 * tt
            op("scalar", lambda e, tt=tt, t0=t0: e.dma_start(
                accT[:, tt, :], x[t0:t0 + 128, :]), wr=[bacc[tt]], dma=f"dA{tt}")
        for lt in range(NLT + 1):
            if lt < NLT:
                ln_stats(lt)
            if lt >= 1:
                ln_trans(lt - 1)
        for ct in range(CT):
            op("vector", lambda e, ct=ct: e.memset(xnT[:, ct, REAL:GW], 0.0), wr=[bxnT[ct]])

        if cfg.get("dbg5") and grp == 0:
            op("sync", lambda e_: e_.dma_start(
                gdbg[:, 4096:4096 + 512], xnT[:, 0, 0:512]), rd=[bxnT[0]], dma="dHH")
        def emit_routing():
            for tt in range(NTT):
                kb = tt % 2
                oc0 = own0 + 128 * tt
                for ct in range(CT):
                    last = ct == CT - 1
                    op("tensor", lambda e, ct=ct, kb=kb, oc0=oc0: e.matmul(
                        pP[kb][:, 0:NE], xnT[:, ct, oc0:oc0 + 128], rwS[:, ct, :],
                        start=(ct == 0), stop=(ct == CT - 1)),
                        rd=[bxnT[ct], bconst] if last else (),
                        wr=[pB[kb]] if last else (),
                        rd_nt=() if last else (bxnT[ct], bconst),
                        war=(pB[kb],) if ct == 0 else (),
                        inc=last)
                op("vector", lambda e, tt=tt, kb=kb: e.tensor_copy(
                    rsc[:, tt, 0:NE], pP[kb][:, 0:NE]), rd=[pB[kb]], wr=[brr[tt][0]])
            E = NE

            def rr(tt, j):
                return rsc[:, tt, j * E:(j + 1) * E]

            def rs(tt, j):
                return rsc[:, tt, 8 * E + j:8 * E + j + 1]
            for tt in range(NTT):
                op("vector", lambda e, tt=tt: e.reduce_max(
                    rs(tt, 0), rr(tt, 0), axis=AX.X, negate=True), rd=[brr[tt][0]], wr=[brr[tt][1]])
            for tt in range(NTT):
                op("vector", lambda e, tt=tt: e.tensor_scalar_add(
                    rr(tt, 1), rr(tt, 0), rs(tt, 0)), rd=[brr[tt][0], brr[tt][1]], wr=[brr[tt][2]])
            for tt in range(NTT):
                op("scalar", lambda e, tt=tt: e.activation(
                    rr(tt, 2), rr(tt, 1), AF.Exp), rd=[brr[tt][2]], wr=[brr[tt][3]])
            for tt in range(NTT):
                op("vector", lambda e, tt=tt: e.tensor_scalar(
                    rr(tt, 3), rr(tt, 1), 0.0, None, ALU.is_equal), rd=[brr[tt][2]], wr=[brr[tt][4]])
            for tt in range(NTT):
                op("vector", lambda e, tt=tt: e.scalar_tensor_tensor(
                    rr(tt, 4), rr(tt, 3), -1e30, rr(tt, 1), ALU.mult, ALU.add),
                    rd=[brr[tt][2], brr[tt][4]], wr=[brr[tt][5]])
            for tt in range(NTT):
                op("vector", lambda e, tt=tt: e.reduce_max(
                    rs(tt, 1), rr(tt, 4), axis=AX.X), rd=[brr[tt][5]], wr=[brr[tt][6]])
            for tt in range(NTT):
                op("vector", lambda e, tt=tt: e.tensor_scalar(
                    rr(tt, 5), rr(tt, 1), rs(tt, 1), None, ALU.is_equal),
                    rd=[brr[tt][2], brr[tt][6]], wr=[brr[tt][7]])
            for tt in range(NTT):
                op("vector", lambda e, tt=tt: e.tensor_add(
                    rr(tt, 6), rr(tt, 3), rr(tt, 5)), rd=[brr[tt][4], brr[tt][7]], wr=[brr[tt][8]])
            for tt in range(NTT):
                op("vector", lambda e, tt=tt: e.tensor_mul(
                    rr(tt, 7), rr(tt, 2), rr(tt, 6)), rd=[brr[tt][3], brr[tt][8]], wr=[brr[tt][9]])
            for tt in range(NTT):
                op("vector", lambda e, tt=tt: e.reduce_sum(
                    rs(tt, 2), rr(tt, 7), axis=AX.X), rd=[brr[tt][9]], wr=[brr[tt][10]])
            for tt in range(NTT):
                op("vector", lambda e, tt=tt: e.reciprocal(
                    rs(tt, 3), rs(tt, 2)), rd=[brr[tt][10]], wr=[brr[tt][11]])
            for tt in range(NTT):
                op("vector", lambda e, tt=tt: e.tensor_scalar_mul(
                    cmbT[:, tt, :], rr(tt, 7), rs(tt, 3)),
                    rd=[brr[tt][9], brr[tt][11]], wr=[bcmb[tt]])

            for tt in range(NTT):
                op("vector", lambda e, tt=tt: e.tensor_copy(
                    indb[:, tt, :], rr(tt, 6)), rd=[brr[tt][8]], wr=[brk[tt][0]])
            for tt in range(NTT):
                kb = 2 + tt % 2
                for tt2 in range(tt):
                    op("tensor", lambda e, tt2=tt2, kb=kb, first=(tt2 == 0): e.matmul(
                        pP[kb][:, 0:NE], alloneS[:, :], indb[:, tt2, :],
                        start=first, stop=False),
                        rd_nt=(bconst, brk[tt2][0]),
                        war=(pB[kb],) if tt2 == 0 else (), inc=False)
                op("tensor", lambda e, tt=tt, kb=kb, first=(tt == 0): e.matmul(
                    pP[kb][:, 0:NE], ltriS[:, :], indb[:, tt, :],
                    start=first, stop=True),
                    rd=[bconst] + [brk[t2][0] for t2 in range(tt + 1)],
                    war=(pB[kb],) if tt == 0 else (), wr=[pB[kb]])
                op("vector", lambda e, tt=tt, kb=kb: e.tensor_copy(
                    rank8[:, tt, :], pP[kb][:, 0:NE]), rd=[pB[kb]], wr=[brk[tt][1]])
            BIGOFF = float(NE * CAP * 16)
            for tt in range(NTT):
                for j in range(2):
                    EQ = rr(tt, 3) if j == 0 else rr(tt, 5)
                    EQb = brr[tt][4] if j == 0 else brr[tt][7]
                    op("vector", lambda e, tt=tt, EQ=EQ: e.tensor_mul(
                        rtmp[:, tt, :], EQ, iotaS[:, :]), rd=[EQb, bconst], wr=[brk[tt][2]])
                    op("vector", lambda e, tt=tt, j=j: e.reduce_sum(
                        rsx[:, tt, 0 + j:1 + j], rtmp[:, tt, :], axis=AX.X),
                        rd=[brk[tt][2]], wr=[brk[tt][3 + j]])
                    op("vector", lambda e, tt=tt, EQ=EQ: e.tensor_mul(
                        rtmp[:, tt, :], EQ, rank8[:, tt, :]),
                        rd=[EQb, brk[tt][1]], wr=[brk[tt][2]])
                    op("vector", lambda e, tt=tt, j=j: e.reduce_sum(
                        rsx[:, tt, 2 + j:3 + j], rtmp[:, tt, :], axis=AX.X),
                        rd=[brk[tt][2]], wr=[brk[tt][5 + j]])
                    op("vector", lambda e, tt=tt, EQ=EQ: e.tensor_mul(
                        rtmp[:, tt, :], EQ, cmbT[:, tt, :]),
                        rd=[EQb, bcmb[tt]], wr=[brk[tt][2]])
                    op("vector", lambda e, tt=tt, j=j: e.reduce_sum(
                        c12[:, tt, j:j + 1], rtmp[:, tt, :], axis=AX.X),
                        rd=[brk[tt][2]], wr=[brk[tt][7 + j]])
                    op("vector", lambda e, tt=tt, j=j: e.tensor_scalar(
                        rsx[:, tt, 4:5], rsx[:, tt, 0 + j:1 + j], float(CAP),
                        rsx[:, tt, 2 + j:3 + j], ALU.mult, ALU.add),
                        rd=[brk[tt][3 + j], brk[tt][5 + j]], wr=[brk[tt][9]])
                    op("vector", lambda e, tt=tt: e.tensor_scalar_min(
                        rsx[:, tt, 5:6], rsx[:, tt, 4:5], float(NE * CAP - 1)),
                        rd=[brk[tt][9]], wr=[brk[tt][10]])
                    op("vector", lambda e, tt=tt, j=j: e.tensor_copy(
                        aidx[:, tt, j:j + 1], rsx[:, tt, 5:6]),
                        rd=[brk[tt][10]], wr=[brk[tt][14]])
                    op("vector", lambda e, tt=tt, j=j: e.tensor_scalar(
                        rsx[:, tt, 6:7], rsx[:, tt, 2 + j:3 + j], float(CAP), None, ALU.is_lt),
                        rd=[brk[tt][5 + j]], wr=[brk[tt][11]])
                    op("vector", lambda e, tt=tt: e.tensor_scalar_add(
                        rsx[:, tt, 7:8], rsx[:, tt, 4:5], -BIGOFF),
                        rd=[brk[tt][9]], wr=[brk[tt][12]])
                    op("vector", lambda e, tt=tt: e.tensor_mul(
                        rsx[:, tt, 7:8], rsx[:, tt, 6:7], rsx[:, tt, 7:8]),
                        rd=[brk[tt][11]], wr=[brk[tt][12]])
                    op("vector", lambda e, tt=tt: e.tensor_scalar_add(
                        rsx[:, tt, 7:8], rsx[:, tt, 7:8], BIGOFF),
                        wr=[brk[tt][12]])
                    op("vector", lambda e, tt=tt, j=j: e.tensor_copy(
                        oidx[:, tt, j:j + 1], rsx[:, tt, 7:8]),
                        rd=[brk[tt][12]], wr=[brk[tt][13]])


        cast_q = [(e, i) for e in range(NE) for i in range(IT)] if grp == 0 else []
        cast_ctr = [0]
        bounce = [ar_bf[:, s:s + C] for s in (0, 2 * C, 4 * C, 5 * C)]
        bBN = [bxt[0], bxt[1], bxns[0], bxns[1]]

        def emit_casts(n):
            while n and cast_q:
                e, i = cast_q.pop(0)
                sl = cast_ctr[0] % 4
                cast_ctr[0] += 1
                op("gpsimd", lambda g, e=e, i=i, sl=sl: g.dma_start(
                    bounce[sl], dwt[e][128 * i:128 * i + 128, :]),
                    wr=[bBN[sl]], dma=f"dD{sl % NDW}")
                op("sync", lambda e_, e=e, i=i, sl=sl: e_.dma_start(
                    dwb[I * e + 128 * i:I * e + 128 * i + 128, :], bounce[sl]),
                    rd=[bBN[sl]], dma=f"dW{sl}")
                n -= 1

        def load_qkv(hp):
            op("gpsimd", lambda g, hp=hp: g.dma_start(
                qkvS[:, 0, :, :], qkvw[hp].rearrange("(ct p) o -> p ct o", p=128)),
                wr=[bqkv[0]], dma="dQ0")

        def load_prj(hp):
            s8 = hp % 6
            op("gpsimd", lambda g, hp=hp, s8=s8: g.dma_start(
                prjS[:, s8, :], projw[128 * hp:128 * hp + 128, :]),
                wr=[bprj[s8]], dma=f"dP{s8}")

        proj_jobs = []

        def make_proj_block(block_hps):
            with_bias = block_hps[0] == 0
            for tt in range(NTT):
                for oc in range(NCH):
                    kb = 4 + (tt * NCH + oc) % 2
                    ol = min(512, C - 512 * oc)
                    n = len(block_hps)
                    for j, hp2 in enumerate(block_hps):
                        sa = hp2 % 8
                        s6 = hp2 % 6
                        first = j == 0
                        closing = (j == n - 1) and not with_bias
                        proj_jobs.append((
                            "tensor",
                            (lambda e, tt=tt, oc=oc, ol=ol, kb=kb, sa=sa, s6=s6,
                                    first=first, closing=closing: e.matmul(
                                pP[kb][:, 0:ol], ahp[:, sa, 128 * tt:128 * tt + 128],
                                prjS[:, s6, 512 * oc:512 * oc + ol],
                                start=first, stop=closing)),
                            dict(rd=[bahp[b2 % 8] for b2 in block_hps] +
                                    [bprj[b2 % 6] for b2 in block_hps],
                                 wr=[pB[kb]]) if closing else
                            dict(rd_nt=(bahp[sa], bprj[s6]),
                                 war=(pB[kb],) if first else (), inc=False)))
                    if with_bias:
                        proj_jobs.append((
                            "tensor",
                            (lambda e, oc=oc, ol=ol, kb=kb: e.matmul(
                                pP[kb][:, 0:ol], ones1[0:1, :],
                                pbrow[0:1, 512 * oc:512 * oc + ol],
                                start=False, stop=True)),
                            dict(rd=[bahp[b2 % 8] for b2 in block_hps] +
                                    [bprj[b2 % 6] for b2 in block_hps] + [bconst],
                                 wr=[pB[kb]])))
                    proj_jobs.append((
                        "vector",
                        (lambda e, tt=tt, oc=oc, ol=ol, kb=kb: e.tensor_add(
                            accT[:, tt, 512 * oc:512 * oc + ol],
                            accT[:, tt, 512 * oc:512 * oc + ol], pP[kb][:, 0:ol])),
                        dict(rd=[pB[kb], bacc[tt]], wr=[bacc[tt]])))

        def drain_proj(nj):
            while nj and proj_jobs:
                eng, fn, kw = proj_jobs.pop(0)
                op(eng, fn, **kw)
                nj -= 1

        HPL = 0 if cfg.get("no_attn") else HP
        if HPL:
            load_qkv(0)
            load_prj(0)
        chunks_kv = [(0, 512), (512, 512), (1024, GW - 1024)]
        chunks_q = [(own0, 512), (own0 + 512, 512)]
        for hp in range(HPL):
            p = hp % 2
            for mi, (dst, bdst) in enumerate(((Qf, bQ), (Kf, bK), (Vf, bV))):
                chunks = chunks_q if mi == 0 else chunks_kv
                for ci, (c0, cl) in enumerate(chunks):
                    kb = (mi * len(chunks) + ci) % 2
                    for ct in range(CT):
                        last = ct == CT - 1
                        op("tensor", lambda e, ct=ct, mi=mi, c0=c0, cl=cl, kb=kb: e.matmul(
                            pP[kb][:, 0:cl], qkvS[:, 0, ct, 128 * mi:128 * mi + 128],
                            xnT[:, ct, c0:c0 + cl], start=(ct == 0), stop=(ct == CT - 1)),
                            rd=[bxnT[ct], bqkv[0]] if last else (),
                            wr=[pB[kb]] if last else (),
                            rd_nt=() if last else (bxnT[ct], bqkv[0]),
                            war=(pB[kb],) if ct == 0 else (),
                            inc=last)
                    op("vector", lambda e, dst=dst, c0=c0, cl=cl, kb=kb: e.tensor_copy(
                        dst[:, c0:c0 + cl], pP[kb][:, 0:cl]), rd=[pB[kb]], wr=[bdst])
            if hp + 1 < HPL:
                load_qkv(hp + 1)
            NU = 2 * NQT

            def uparams(k):
                qt, hh = divmod(k, 2)
                if grp == 0:
                    w0, mid = (0, 1) if qt == 0 else (128 * qt - 8, 0)
                else:
                    w0, mid = (128 * qt, 0) if qt < NQT - 1 else (128 * qt, 2)
                return qt, hh, w0, mid, own0 + 128 * qt, 64 * hh

            def emit_vw(qt):
                qt_, hh_, w0, mid, q0, d0 = uparams(2 * qt)
                tp = qt % 4
                kbt = 0 + qt % 2
                op("tensor", lambda e, w0=w0, kbt=kbt: e.transpose(
                    pbf(kbt)[:, 0:128], Vf[:, w0:w0 + 128], idb[:, :]),
                    rd=[bV, bconst], war=[pB[kbt]], wr=[pB[kbt]])
                op("tensor", lambda e, w0=w0, kbt=kbt: e.transpose(
                    pbf(kbt)[0:16, 128:256], Vf[:, w0 + 128:w0 + WIN], idb[:, :]),
                    rd=[bV, bconst], wr=[pB[kbt]])
                op("vector", lambda e, kbt=kbt, tp=tp: e.tensor_copy(
                    vwa[:, tp, :], pbf(kbt)[:, 0:128]), rd=[pB[kbt]], wr=[bvwa[tp]])
                op("vector", lambda e, kbt=kbt, tp=tp: e.tensor_copy(
                    vwb[0:16, tp, :], pbf(kbt)[0:16, 128:256]), rd=[pB[kbt]], wr=[bvwb[tp]])

            def emit_scores(k):
                qt, hh, w0, mid, q0, d0 = uparams(k)
                kbs = 2 + k % 2
                op("tensor", lambda e, q0=q0, w0=w0, d0=d0, kbs=kbs: e.matmul(
                    pP[kbs][:, 0:WIN], Qf[d0:d0 + 64, q0:q0 + 128],
                    Kf[d0:d0 + 64, w0:w0 + WIN], start=True, stop=True),
                    rd=[bQ, bK], war=[pB[kbs]], wr=[pB[kbs]])

            def emit_mask(k):
                qt, hh, w0, mid, q0, d0 = uparams(k)
                kbs = 2 + k % 2
                u = k % 4
                u3 = k % 3
                op("vector", lambda e, u3=u3, kbs=kbs, mid=mid: e.tensor_mul(
                    scm[:, u3, :], pP[kbs][:, 0:WIN], mcs[:, mid, :]),
                    rd=[pB[kbs], bconst], wr=[bscm[u3]])
                op("vector", lambda e, u3=u3, mid=mid: e.tensor_add(
                    scm[:, u3, :], scm[:, u3, :], mas[:, mid, :]),
                    rd=[bconst], wr=[bscm[u3]])
                op("vector", lambda e, u=u, u3=u3: e.reduce_max(
                    sst[:, u, 0:1], scm[:, u3, :], axis=AX.X, negate=True),
                    rd=[bscm[u3]], wr=[bsst[u]])

            def emit_exp(k):
                u = k % 4
                u3 = k % 3
                op("scalar", lambda e, u=u, u3=u3: e.activation(
                    exv[:, u, :], scm[:, u3, :], AF.Exp, bias=sst[:, u, 0:1],
                    accum_out=sst[:, u, 1:2]), rd=[bscm[u3], bsst[u]], wr=[bexv[u], bsst[u]])

            def emit_scale(k):
                u = k % 4
                op("vector", lambda e, u=u: e.reciprocal(
                    sst[:, u, 2:3], sst[:, u, 1:2]), rd=[bsst[u]], wr=[bsst[u]])
                op("vector", lambda e, u=u: e.tensor_scalar_mul(
                    abf[:, u, :], exv[:, u, :], sst[:, u, 2:3]),
                    rd=[bsst[u], bexv[u]], wr=[babf[u]])

            def emit_trans(k):
                u = k % 4
                kb2 = 6 + k % 2
                op("tensor", lambda e, u=u, kb2=kb2: e.transpose(
                    pbf(kb2)[:, 0:128], abf[:, u, 0:128], idb[:, :]),
                    rd=[babf[u], bconst], war=[pB[kb2]], wr=[pB[kb2]])
                op("tensor", lambda e, u=u, kb2=kb2: e.transpose(
                    pbf(kb2)[0:16, 128:256], abf[:, u, 128:WIN], idb[:, :]),
                    rd=[babf[u], bconst], wr=[pB[kb2]])
                op("vector", lambda e, u=u, kb2=kb2: e.tensor_copy(
                    aTa[:, u, :], pbf(kb2)[:, 0:128]), rd=[pB[kb2]], wr=[baTa[u]])
                op("vector", lambda e, u=u, kb2=kb2: e.tensor_copy(
                    aTb[0:16, u, :], pbf(kb2)[0:16, 128:256]), rd=[pB[kb2]], wr=[baTb[u]])

            def emit_av(k, p):
                qt, hh, w0, mid, q0, d0 = uparams(k)
                u = k % 4
                kba = 6 + k % 2
                tp = qt % 4
                s8 = hp % 8
                op("tensor", lambda e, u=u, kba=kba, d0=d0, tp=tp: e.matmul(
                    pP[kba][0:64, 256:384], vwa[:, tp, d0:d0 + 64], aTa[:, u, :],
                    start=True, stop=False),
                    rd_nt=(bvwa[tp], baTa[u]), war=(pB[kba],), inc=False)
                op("tensor", lambda e, u=u, kba=kba, d0=d0, tp=tp: e.matmul(
                    pP[kba][0:64, 256:384], vwb[0:16, tp, d0:d0 + 64], aTb[0:16, u, :],
                    start=False, stop=True),
                    rd=[bvwa[tp], bvwb[tp], baTa[u], baTb[u]], wr=[pB[kba]])
                op("vector", lambda e, u=u, kba=kba, hh=hh, qt=qt, s8=s8: e.tensor_copy(
                    ahp[64 * hh:64 * hh + 64, s8, 128 * qt:128 * qt + 128],
                    pP[kba][0:64, 256:384]), rd=[pB[kba]], wr=[bahp[s8]])

            for step in range(NU + 6):
                if 6 <= step <= NU + 5:
                    emit_av(step - 6, p)
                if 5 <= step <= NU + 4:
                    emit_trans(step - 5)
                if 3 <= step <= NU + 2:
                    emit_scale(step - 3)
                if 2 <= step <= NU + 1:
                    emit_exp(step - 2)
                if 1 <= step <= NU:
                    emit_mask(step - 1)
                if step < NU:
                    if step % 2 == 0:
                        emit_vw(step // 2)
                    emit_scores(step)
                drain_proj(4)
            if hp % 4 == 3 or hp == HPL - 1:
                make_proj_block(list(range(4 * (hp // 4), hp + 1)))
            if hp + 1 < HPL:
                load_prj(hp + 1)
            emit_casts(6)
        drain_proj(len(proj_jobs))
        emit_casts(len(cast_q))
        emit_routing()

        if cfg.get("no_moe"):
            for tt in range(NTT):
                t0 = grp * GT + 128 * tt
                op("sync", lambda e_, tt=tt, t0=t0: e_.dma_start(
                    y[t0:t0 + 128, :], accT[:, tt, :]), rd=[bacc[tt]], dma=f"dO{tt}")
            b.fence(LNATTN_BUFS, LNATTN_BUFS)
            continue
        if cfg.get("dbg5") and grp == 0:
            op("sync", lambda e_: e_.dma_start(
                gdbg[:, 4608:4608 + 512], xnT[:, 0, 0:512]), rd=[bxnT[0]], dma="dHI")
        b.fence(LNATTN_BUFS, MOE_BUFS)
        QC = C // 4
        for tt in range(NTT):
            kx = tt % 2
            op("sync", lambda e_, tt=tt, kx=kx, own0=own0: e_.dma_start(
                xe[:, kx, :], xtm[own0 + 128 * tt:own0 + 128 * tt + 128, :]),
                rd=[bxtm], wr=[bxe[kx]], dma=f"dXE{kx}")
            for j in range(2):
                op("gpsimd", lambda g, tt=tt, j=j, kx=kx: g.indirect_dma_start(
                    xdram[:, :], bass.IndirectOffsetOnAxis(oidx[:, tt, j:j + 1], 0),
                    xe[:, kx, :], None,
                    bounds_check=NE * CAP - 1, oob_is_err=False),
                    rd=[bxe[kx], brk[tt][13]], wr=[bxdram], dma="dSC")
        if cfg.get("dbg5") and grp == 0:
            op("sync", lambda e_: e_.dma_start(
                gdbg[:, 5120:5120 + 512], xnT[:, 0, 0:512]), rd=[bxnT[0]], dma="dHJ")
        especs = [("e", e, None) for e in range(NE)] + \
                 [("s", sh, th) for sh in range(2) for th in range(2)]
        if cfg.get("dbg1"):
            especs = especs[:NE]

        def gu_dma(spec, it, pair):
            kind, e, th = spec
            if kind == "e":
                srcg = gw[e][:, 128 * it:128 * it + 128]
                srcu = uwt[e][:, 128 * it:128 * it + 128]
            else:
                srcg = sgw[:, SIH * e + 128 * it:SIH * e + 128 * it + 128]
                srcu = suw[:, SIH * e + 128 * it:SIH * e + 128 * it + 128]
            s0 = (2 * pair) % NGU
            s1 = (2 * pair + 1) % NGU
            op("gpsimd", lambda g, srcg=srcg, s0=s0: g.dma_start(
                guS[:, s0, :, :], srcg.rearrange("(ct p) i -> p ct i", p=128)),
                wr=[bgu[s0]], dma=f"dG{s0}")
            op("gpsimd", lambda g, srcu=srcu, s1=s1: g.dma_start(
                guS[:, s1, :, :], srcu.rearrange("(ct p) i -> p ct i", p=128)),
                wr=[bgu[s1]], dma=f"dG{s1}")

        dwctr = [0]

        def dw_dma(spec, it):
            kind, e, th = spec
            if kind == "e":
                srcd = dwb[I * e + 128 * it:I * e + 128 * it + 128, :]
            else:
                srcd = sdw[SIH * e + 128 * it:SIH * e + 128 * it + 128, :]
            sl = dwctr[0] % NDW
            dwctr[0] += 1
            op("gpsimd", lambda g, srcd=srcd, sl=sl: g.dma_start(
                dwS[:, sl, :], srcd), wr=[bdw[sl]], dma=f"dD{sl}")
            return sl

        gu_q = [(spec, it) for spec in especs for it in range((IT if spec[0] == "e" else SIT))]
        gu_next = 0
        for _ in range(3):
            if gu_next < len(gu_q):
                gu_dma(gu_q[gu_next][0], gu_q[gu_next][1], gu_next)
                gu_next += 1

        for si, spec in enumerate(especs):
            kind, e, th = spec
            nit = IT if kind == "e" else SIT
            if kind == "e":
                for k in range(4):
                    kx = k % 2
                    op("sync", lambda e_, e=e, k=k, kx=kx: e_.dma_start(
                        xe[:, kx, :], xdram[e * CAP + 128 * k:e * CAP + 128 * k + 128, :]),
                        rd=[bxdram], wr=[bxe[kx]], dma=f"dXE{kx}")
                    for ct in range(CT):
                        kb = 4 + ct % 2
                        op("tensor", lambda e_, ct=ct, kx=kx, kb=kb: e_.transpose(
                            pbf(kb)[:, 0:128], xe[:, kx, 128 * ct:128 * ct + 128], idb[:, :]),
                            rd=[bxe[kx], bconst], war=[pB[kb]], wr=[pB[kb]])
                        op("vector", lambda e_, ct=ct, k=k, kb=kb: e_.tensor_copy(
                            xeT[:, ct, 128 * k:128 * k + 128], pbf(kb)[:, 0:128]),
                            rd=[pB[kb]], wr=[bxeT])
            if cfg.get("dbg5") and grp == 0 and spec == ("s", 0, 0):
                op("sync", lambda e_: e_.dma_start(
                    gdbg[:, 0:CT * 128], guS[:, 0, :, :].rearrange("p ct i -> p (ct i)")),
                    rd=[bgu[0]], dma="dHE")
                op("sync", lambda e_: e_.dma_start(
                    gdbg[:, 1024:1024 + CT * 128], guS[:, 1, :, :].rearrange("p ct i -> p (ct i)")),
                    rd=[bgu[1]], dma="dHF")
                for ct in range(CT):
                    op("sync", lambda e_, ct=ct, own0=own0: e_.dma_start(
                        gdbg[:, 2048 + ct * 512:2048 + ct * 512 + 512],
                        xnT[:, ct, own0:own0 + 512]), rd=[bxnT[ct]], dma=f"dHG{ct}")
            if cfg.get("dbg5") and grp == 0 and kind == "e" and e in (0, 3, 7):
                c0d = 5632 + 512 * (0 if e == 0 else (1 if e == 3 else 2))
                op("sync", lambda e_, c0d=c0d: e_.dma_start(
                    gdbg[:, c0d:c0d + 512], xnT[:, 0, 0:512]), rd=[bxnT[0]], dma=f"dHK{e}")
            for it in range(nit):
                pair = (sum((IT if s[0] == "e" else SIT) for s in especs[:si]) + it)
                s0 = (2 * pair) % NGU
                s1 = (2 * pair + 1) % NGU
                kbg = (it % 2) * 2
                kbu = kbg + 1
                for ct in range(CT):
                    last = ct == CT - 1
                    mv_ap = (lambda e_, ct=ct, s0=s0, kbg=kbg, kind=kind, th=th, own0=own0: e_.matmul(
                        pP[kbg][:, :],
                        guS[:, s0, ct, :],
                        xeT[:, ct, :] if kind == "e" else xnT[:, ct, own0 + 512 * th:own0 + 512 * th + 512],
                        start=(ct == 0), stop=(ct == CT - 1)))
                    mvb = bxeT if kind == "e" else bxnT[ct]
                    op("tensor", mv_ap,
                        rd=[bgu[s0], mvb] if last else (),
                        wr=[pB[kbg]] if last else (),
                        rd_nt=() if last else (bgu[s0], mvb),
                        war=(pB[kbg],) if ct == 0 else (),
                        inc=last)
                for ct in range(CT):
                    last = ct == CT - 1
                    mv_ap = (lambda e_, ct=ct, s1=s1, kbu=kbu, kind=kind, th=th, own0=own0: e_.matmul(
                        pP[kbu][:, :],
                        guS[:, s1, ct, :],
                        xeT[:, ct, :] if kind == "e" else xnT[:, ct, own0 + 512 * th:own0 + 512 * th + 512],
                        start=(ct == 0), stop=(ct == CT - 1)))
                    mvb = bxeT if kind == "e" else bxnT[ct]
                    op("tensor", mv_ap,
                        rd=[bgu[s1], mvb] if last else (),
                        wr=[pB[kbu]] if last else (),
                        rd_nt=() if last else (bgu[s1], mvb),
                        war=(pB[kbu],) if ct == 0 else (),
                        inc=last)
                sp = it % 2
                if kind == "e":
                    op("scalar", lambda e_, sp=sp, kbg=kbg: e_.activation(
                        sgS[:, sp, :], pP[kbg][:, :], AF.Silu), rd=[pB[kbg]], wr=[bsg[sp]])
                    op("vector", lambda e_, it=it, sp=sp, kbu=kbu: e_.tensor_mul(
                        hS[:, it, :], sgS[:, sp, :], pP[kbu][:, :]),
                        rd=[bsg[sp], pB[kbu]], wr=[bh[it]])
                else:
                    bidx = SIT * e + it
                    op("scalar", lambda e_, sp=sp, kbg=kbg, bidx=bidx: e_.activation(
                        sgS[:, sp, :], pP[kbg][:, :], AF.Silu, bias=sgbC[:, bidx:bidx + 1]),
                        rd=[pB[kbg], bconst], wr=[bsg[sp]])
                    op("vector", lambda e_, it=it, sp=sp, kbu=kbu, bidx=bidx: e_.scalar_tensor_tensor(
                        hS[:, it, :], pP[kbu][:, :],
                        subC[:, bidx:bidx + 1], sgS[:, sp, :], ALU.add, ALU.mult),
                        rd=[bsg[sp], pB[kbu], bconst], wr=[bh[it]])
                if gu_next < len(gu_q):
                    gu_dma(gu_q[gu_next][0], gu_q[gu_next][1], gu_next)
                    gu_next += 1
            if cfg.get("dbg5") and grp == 0 and spec == ("s", 0, 0):
                op("sync", lambda e_: e_.dma_start(
                    hdbg[:, 0:SIT * 512], hS[:, 0:SIT, :]), rd=[bh[i] for i in range(SIT)], dma="dHD")
            if cfg.get("dbg5") and grp == 0 and spec == ("e", 7, None):
                op("sync", lambda e_: e_.dma_start(
                    gdbg[:, 7168:7168 + 512], xnT[:, 0, 0:512]), rd=[bxnT[0]], dma="dHL")
            bias_close = (kind == "s" and e == 0)
            for och in range(2):
                slq = [dw_dma(spec, i) for i in range(min(nit, 4))]
                for it in range(nit):
                    if it + 4 < nit:
                        slq.append(dw_dma(spec, it + 4))
                    sl = slq[it]
                    lastit = it == nit - 1
                    for k in range(4):
                        for oc in range(2):
                            col = (och * 2 + oc) * QC
                            last = lastit and not bias_close
                            trk = (k == 3 and oc == 1) or last
                            op("tensor", lambda e_, it=it, k=k, oc=oc, col=col, sl=sl, last=last: e_.matmul(
                                pP[2 * k + oc][:, 0:QC], hS[:, it, 128 * k:128 * k + 128],
                                dwS[:, sl, col:col + QC],
                                start=(it == 0), stop=last),
                                rd=[bh[it], bdw[sl]] if trk else (),
                                wr=[pB[2 * k + oc]] if last else (),
                                rd_nt=() if trk else (bh[it], bdw[sl]),
                                war=(pB[2 * k + oc],) if it == 0 else (),
                                inc=trk)
                if bias_close:
                    for k in range(4):
                        for oc in range(2):
                            col = (och * 2 + oc) * QC
                            op("tensor", lambda e_, k=k, oc=oc, col=col: e_.matmul(
                                pP[2 * k + oc][:, 0:QC], ones1[32:33, :],
                                sdbrow[0:1, col:col + QC], start=False, stop=True),
                                rd=[bconst] + [bh[i] for i in range(nit)],
                                wr=[pB[2 * k + oc]])
                for k in range(4):
                    if kind == "e":
                        kx = k % 2
                        for oc in range(2):
                            col = (och * 2 + oc) * QC
                            op("vector", lambda e_, k=k, oc=oc, col=col, kx=kx: e_.tensor_copy(
                                stg[:, kx, col:col + QC], pP[2 * k + oc][:, 0:QC]),
                                rd=[pB[2 * k + oc]], wr=[bstg[kx]])
                        op("sync", lambda e_, e=e, k=k, och=och, kx=kx: e_.dma_start(
                            ddram[e * CAP + 128 * k:e * CAP + 128 * k + 128,
                                  och * 2 * QC:och * 2 * QC + 2 * QC],
                            stg[:, kx, och * 2 * QC:och * 2 * QC + 2 * QC]),
                            rd=[bstg[kx]], wr=[bddram], dma=f"dDD{kx}")
                    else:
                        tt = th * 4 + k
                        for oc in range(2):
                            col = (och * 2 + oc) * QC
                            op("vector", lambda e_, tt=tt, k=k, oc=oc, col=col: e_.tensor_add(
                                accT[:, tt, col:col + QC],
                                accT[:, tt, col:col + QC], pP[2 * k + oc][:, 0:QC]),
                                rd=[pB[2 * k + oc], bacc[tt]], wr=[bacc[tt]])
            if cfg.get("dbg5") and grp == 0 and spec == ("e", 7, None):
                op("sync", lambda e_: e_.dma_start(
                    gdbg[:, 7680:7680 + 512], xnT[:, 0, 0:512]), rd=[bxnT[0]], dma="dHM")
            if kind == "e" and e == NE - 1 and not cfg.get("dbg2"):
                for tt in range(NTT):
                    for j in range(2):
                        kx = j % 2
                        op("gpsimd", lambda g, tt=tt, j=j, kx=kx: g.indirect_dma_start(
                            xe[:, kx, :], None, ddram[:, :],
                            bass.IndirectOffsetOnAxis(aidx[:, tt, j:j + 1], 0)),
                            rd=[bddram, brk[tt][14]], wr=[bxe[kx]], dma=f"dGA{kx}")
                        op("vector", lambda e_, tt=tt, j=j, kx=kx: e_.scalar_tensor_tensor(
                            accT[:, tt, :], xe[:, kx, :], c12[:, tt, j:j + 1],
                            accT[:, tt, :], ALU.mult, ALU.add),
                            rd=[bxe[kx], brk[tt][7 + j], bacc[tt]], wr=[bacc[tt]])
            if cfg.get("dbg5") and grp == 0 and spec == ("e", 7, None):
                op("sync", lambda e_: e_.dma_start(
                    gdbg[:, 8192:8192 + 512], xnT[:, 0, 0:512]), rd=[bxnT[0]], dma="dHN")
            if (kind == "s" and e == 1 and th == 1) or (cfg.get("dbg1") and si == len(especs) - 1):
                for tt in range(NTT):
                    t0 = grp * GT + 128 * tt
                    op("sync", lambda e_, tt=tt, t0=t0: e_.dma_start(
                        y[t0:t0 + 128, :], accT[:, tt, :]), rd=[bacc[tt]], dma=f"dO{tt}")

    fin = []
    for tt in range(NTT):
        name = f"dO{tt}"
        if name in b.sems:
            fin.append((b.sems[name], b.cnt[name]))
    b.plan["sync"].append((lambda e: e.nop(), fin, None, False))
    b.build_block()
    return b


def prep_inputs(L0_attn_x, L0_attn_qkv_w, L0_attn_proj_w, L0_attn_proj_b,
                L0_ffn_router_w, L0_ffn_gate_w, L0_ffn_up_w, L0_ffn_down_w,
                L0_ffn_s_gate_w, L0_ffn_s_gate_b, L0_ffn_s_up_w, L0_ffn_s_up_b,
                L0_ffn_s_down_w, L0_ffn_s_down_b, cfg):
    C, HP = cfg["C"], cfg["HEADS"] // 2
    qkv = np.asarray(L0_attn_qkv_w, np.float32)
    qkvw = np.empty((HP, C, 384), np.float32)
    for hp in range(HP):
        blk = np.concatenate([qkv[128 * hp:128 * hp + 128],
                              qkv[C + 128 * hp:C + 128 * hp + 128],
                              qkv[2 * C + 128 * hp:2 * C + 128 * hp + 128]], 0)
        qkvw[hp] = np.ascontiguousarray(blk.T)
    mc, ma = make_masks(C // cfg["HEADS"])
    shared = {
        "qkvw": qkvw,
        "projw": np.ascontiguousarray(np.asarray(L0_attn_proj_w, np.float32).T),
        "rw": np.ascontiguousarray(np.asarray(L0_ffn_router_w, np.float32).T),
        "gw": np.ascontiguousarray(np.asarray(L0_ffn_gate_w, np.float32)),
        "uw": np.ascontiguousarray(np.asarray(L0_ffn_up_w, np.float32)),
        "dw": np.ascontiguousarray(np.asarray(L0_ffn_down_w, np.float32)),
        "sgw": np.ascontiguousarray(np.asarray(L0_ffn_s_gate_w, np.float32)),
        "suw": np.ascontiguousarray(np.asarray(L0_ffn_s_up_w, np.float32)),
        "sdw": np.ascontiguousarray(np.asarray(L0_ffn_s_down_w, np.float32)),
        "pb": np.ascontiguousarray(np.asarray(L0_attn_proj_b, np.float32)),
        "sgb": np.ascontiguousarray(np.asarray(L0_ffn_s_gate_b, np.float32)),
        "sub": np.ascontiguousarray(np.asarray(L0_ffn_s_up_b, np.float32)),
        "sdb": np.ascontiguousarray(np.asarray(L0_ffn_s_down_b, np.float32)),
        "ident": np.eye(128, dtype=np.float32),
        "ltri": np.tril(np.ones((128, 128), np.float32), -1).T,
        "iota8": np.broadcast_to(np.arange(cfg["NE"], dtype=np.float32), (128, cfg["NE"])).copy(),
        "mc": mc,
        "ma": ma,
    }
    xs = np.ascontiguousarray(np.asarray(L0_attn_x, np.float32))
    return xs, shared


def kernel(**inputs):
    cfg = FULL_CFG
    xs, shared = prep_inputs(cfg=cfg, **inputs)
    Bn = xs.shape[0]
    bld = build_program(cfg)
    in_maps = [dict(shared, x=xs[bb]) for bb in range(Bn)]
    try:
        res = run_bass_kernel_spmd(bld.nc, in_maps, core_ids=list(range(Bn)))
    finally:
        bld.stack.close()
    return np.stack([np.asarray(r["y"], np.float32) for r in res.results], 0)


# version 17
# speedup vs baseline: 1.2034x; 1.0493x over previous
import math
from contextlib import ExitStack

import numpy as np

import concourse.bass as bass
import concourse.mybir as mybir
from concourse.bass_utils import run_bass_kernel_spmd

F32 = mybir.dt.float32
BF16 = mybir.dt.bfloat16
I32 = mybir.dt.int32
AF = mybir.ActivationFunctionType
ALU = mybir.AluOpType
AX = mybir.AxisListType

ENGS = ("tensor", "vector", "scalar", "gpsimd", "sync")
ESEM = {"tensor": "eT", "vector": "eV", "scalar": "eS", "gpsimd": "eG", "sync": "eY"}

FULL_CFG = dict(C=2048, HEADS=32, NE=8, I=1408, SIH=1408)
N_TOK = 2048
GT = 1024
REAL = 1032
GW = 1040
WIN = 144
NQT = GT // 128
NTT = GT // 128
NLT = (REAL + 127) // 128


def make_masks(hd):
    scale = hd ** -0.5
    mc = np.zeros((3, 128, WIN), np.float32)
    ma = np.full((3, 128, WIN), -1e30, np.float32)
    for p in range(128):
        for jp in range(p, p + 16):
            d = abs(p + 8 - jp)
            mc[0, p, jp] = scale * (1 - d / 16.0)
            ma[0, p, jp] = 0.0
        for jp in range(max(0, p - 8), p + 8):
            d = abs(p - jp)
            mc[1, p, jp] = scale * (1 - d / 16.0)
            ma[1, p, jp] = 0.0
        for jp in range(p, min(p + 16, 136)):
            d = abs(p + 8 - jp)
            mc[2, p, jp] = scale * (1 - d / 16.0)
            ma[2, p, jp] = 0.0
    return mc, ma


class Buf:
    __slots__ = ("w", "r")

    def __init__(self):
        self.w = None
        self.r = []


class Bld:
    def __init__(self):
        self.nc = bass.Bass("TRN2", target_bir_lowering=False)
        self.stack = ExitStack()
        self.plan = {e: [] for e in ENGS}
        self.sems = {}
        self.cnt = {}
        self.wm = {}

    def sem(self, name):
        if name not in self.sems:
            self.sems[name] = self.stack.enter_context(self.nc.semaphore(name))
            self.cnt[name] = 0
        return self.sems[name]

    def op(self, eng, fn, rd=(), wr=(), war=(), rd_nt=(), inc=True, dma=None):
        waits = {}

        def need(tok):
            if tok is not None and tok[1] > waits.get(tok[0], 0):
                waits[tok[0]] = tok[1]

        for b in rd:
            need(b.w)
        for b in rd_nt:
            need(b.w)
        for b in war:
            need(b.w)
            for t in b.r:
                need(t)
        for b in wr:
            need(b.w)
            for t in b.r:
                need(t)
        real = []
        for s, v in waits.items():
            k = (eng, s)
            if dma is not None or self.wm.get(k, 0) < v:
                real.append((self.sem(s), v))
                if self.wm.get(k, 0) < v:
                    self.wm[k] = v
        tok = None
        if dma is not None:
            self.sem(dma)
            self.cnt[dma] += 16
            tok = (dma, self.cnt[dma])
        elif inc:
            s = ESEM[eng]
            self.sem(s)
            self.cnt[s] += 1
            tok = (s, self.cnt[s])
        self.plan[eng].append((fn, real, self.sems[tok[0]] if tok else None, bool(dma)))
        if tok is not None:
            for b in wr:
                b.w = tok
                b.r = []
            for b in rd:
                b.r.append(tok)
        else:
            assert not wr and not rd
        return tok

    def fence(self, outgoing, incoming):
        fl = [(s, v) for s, v in self.cnt.items() if v > 0]
        for b in incoming:
            b.r = b.r + [t for t in fl if t != b.w]

    def build_block(self):
        with self.nc.Block() as block:
            def run(eng_obj, lst):
                for fn, waits, incsem, isdma in lst:
                    for s, v in waits:
                        eng_obj.wait_ge(s, v)
                    ins = fn(eng_obj)
                    if incsem is not None:
                        ins.then_inc(incsem, 16 if isdma else 1)
            block.tensor(lambda e: run(e, self.plan["tensor"]))
            block.vector(lambda e: run(e, self.plan["vector"]))
            block.scalar(lambda e: run(e, self.plan["scalar"]))
            block.gpsimd(lambda e: run(e, self.plan["gpsimd"]))
            block.sync(lambda e: run(e, self.plan["sync"]))


def build_program(cfg):
    C, HEADS, NE, I, SIH = cfg["C"], cfg["HEADS"], cfg["NE"], cfg["I"], cfg["SIH"]
    HD = C // HEADS
    assert HD == 64
    CT = C // 128
    HP = HEADS // 2
    IT = I // 128
    SIT = SIH // 128
    NCH = math.ceil(C / 512)
    NBN = math.ceil(C / 512)
    b = Bld()
    nc = b.nc
    op = b.op

    x = nc.dram_tensor("x", [N_TOK, C], F32, kind="ExternalInput").ap()
    qkvw = nc.dram_tensor("qkvw", [HP, C, 384], F32, kind="ExternalInput").ap()
    projw = nc.dram_tensor("projw", [C, C], F32, kind="ExternalInput").ap()
    rw = nc.dram_tensor("rw", [C, NE], F32, kind="ExternalInput").ap()
    gw = nc.dram_tensor("gw", [NE, C, I], F32, kind="ExternalInput").ap()
    uwt = nc.dram_tensor("uw", [NE, C, I], F32, kind="ExternalInput").ap()
    dwt = nc.dram_tensor("dw", [NE, I, C], F32, kind="ExternalInput").ap()
    sgw = nc.dram_tensor("sgw", [C, 2 * SIH], F32, kind="ExternalInput").ap()
    suw = nc.dram_tensor("suw", [C, 2 * SIH], F32, kind="ExternalInput").ap()
    sdw = nc.dram_tensor("sdw", [2 * SIH, C], F32, kind="ExternalInput").ap()
    pbd = nc.dram_tensor("pb", [C], F32, kind="ExternalInput").ap()
    sgbd = nc.dram_tensor("sgb", [2 * SIH], F32, kind="ExternalInput").ap()
    subd = nc.dram_tensor("sub", [2 * SIH], F32, kind="ExternalInput").ap()
    sdbd = nc.dram_tensor("sdb", [C], F32, kind="ExternalInput").ap()
    mcd = nc.dram_tensor("mc", [3, 128, WIN], F32, kind="ExternalInput").ap()
    mad = nc.dram_tensor("ma", [3, 128, WIN], F32, kind="ExternalInput").ap()
    idd = nc.dram_tensor("ident", [128, 128], F32, kind="ExternalInput").ap()
    ltrid = nc.dram_tensor("ltri", [128, 128], F32, kind="ExternalInput").ap()
    iotad = nc.dram_tensor("iota8", [128, NE], F32, kind="ExternalInput").ap()
    y = nc.dram_tensor("y", [N_TOK, C], F32, kind="ExternalOutput").ap()
    hdbg = nc.dram_tensor("hdbg", [128, 4096], BF16, kind="ExternalOutput").ap() if cfg.get("dbg5") else None
    gdbg = nc.dram_tensor("gdbg", [128, 12288], BF16, kind="ExternalOutput").ap() if cfg.get("dbg5") else None
    CAP = 512
    xtm = nc.dram_tensor("xtm", [REAL, C], BF16).ap()
    dwb = nc.dram_tensor("dwb", [NE * I, C], BF16).ap()
    sdwb = nc.dram_tensor("sdwb", [2 * SIH, C], BF16).ap()
    gwb = nc.dram_tensor("gwb", [NE * C, I], BF16).ap()
    uwb = nc.dram_tensor("uwb", [NE * C, I], BF16).ap()
    xdram = nc.dram_tensor("xdram", [NE * CAP, C], BF16).ap()
    ddram = nc.dram_tensor("ddram", [NE * CAP, C], BF16).ap()

    A = nc.alloc_sbuf_tensor
    xnT = A("xnT", [128, CT, GW], BF16)
    accT = A("accT", [128, NTT, C], F32)
    mcs = A("mcs", [128, 3, WIN], F32)
    mas = A("mas", [128, 3, WIN], F32)
    idb = A("idb", [128, 128], BF16)
    idf = A("idf", [128, 128], F32)
    ones1 = A("ones1", [33, 128], BF16)
    brow = A("brow", [33, C], BF16)
    pbrow = brow[0:1, :]
    sdbrow = brow[32:33, :]
    sgbC = A("sgbC", [128, 2 * SIT], F32)
    subC = A("subC", [128, 2 * SIT], F32)
    rwS = A("rwS", [128, CT, NE], BF16)
    cmbT = A("cmbT", [128, NTT, NE], F32)
    rsc = A("rsc", [128, NTT, 8 * NE + 4], F32)
    epsb = A("epsb", [128, 1], F32)
    ltriS = A("ltriS", [128, 128], BF16)
    alloneS = A("alloneS", [128, 128], BF16)
    iotaS = A("iotaS", [128, NE], F32)
    indb = A("indb", [128, NTT, NE], BF16)
    rank8 = A("rank8", [128, NTT, NE], F32)
    rtmp = A("rtmp", [128, NTT, NE], F32)
    rsx = A("rsx", [128, NTT, 8], F32)
    oidx = A("oidx", [128, NTT, 2], I32)
    aidx = A("aidx", [128, NTT, 2], I32)
    c12 = A("c12", [128, NTT, 2], F32)
    bno = A("bno", [128, 2, 6 * NBN], F32)
    mv = A("mv", [128, 2, 2], F32)
    lns = A("lns", [128, 2, 4], F32)

    NGU = 6
    LN_SPECS = [
        ("xt_", [128, 2, C], F32), ("xns", [128, 2, C], BF16),
        ("qkvS", [128, 1, CT, 384], BF16), ("prjS", [128, 6, C], BF16),
        ("Qf", [128, GW], BF16), ("Kf", [128, GW], BF16), ("Vf", [128, GW], BF16),
        ("ahp", [128, 8, GT], BF16), ("scm", [128, 3, WIN], F32),
        ("exv", [128, 4, WIN], F32), ("sst", [128, 4, 4], F32),
        ("abf", [128, 4, WIN], BF16),
        ("aTa", [128, 4, 128], BF16), ("aTb", [128, 4, 128], BF16),
        ("vwa", [128, 4, 128], BF16), ("vwb", [128, 4, 128], BF16),
    ]
    NDW = 5
    MOE_SPECS = [
        ("guS", [128, NGU, CT, 128], BF16), ("dwS", [128, NDW, C], BF16),
        ("hS", [128, max(IT, SIT), 512], BF16), ("sgS", [128, 2, 512], BF16),
        ("xe", [128, 2, C], BF16), ("xeT", [128, CT, 512], BF16),
        ("stg", [128, 2, C], BF16),
    ]

    def _sz(specs):
        tot = 0
        for _, shape, dt in specs:
            n = 1
            for s in shape[1:]:
                n *= s
            tot += n if dt == F32 else (n + 1) // 2
        return tot

    ARE = max(_sz(LN_SPECS), _sz(MOE_SPECS)) + 16
    arena = A("arena", [128, ARE], F32)
    ar_f32 = arena.ap()
    ar_bf = arena.ap().bitcast(BF16)

    def carve(specs):
        out = {}
        off = 0
        for name, shape, dt in specs:
            n = 1
            for s in shape[1:]:
                n *= s
            if dt == F32:
                apv = ar_f32[:, off:off + n]
                off += n
            else:
                apv = ar_bf[:, 2 * off:2 * off + n]
                off += (n + 1) // 2
            if len(shape) == 3:
                apv = apv.rearrange("p (a b) -> p a b", a=shape[1])
            elif len(shape) == 4:
                apv = apv.rearrange("p (a b c) -> p a b c", a=shape[1], b=shape[2])
            out[name] = apv
        return out

    V1 = carve(LN_SPECS)
    V2 = carve(MOE_SPECS)
    xt_, xns, qkvS, prjS = V1["xt_"], V1["xns"], V1["qkvS"], V1["prjS"]
    Qf, Kf, Vf, ahp = V1["Qf"], V1["Kf"], V1["Vf"], V1["ahp"]
    scm, exv, sst = V1["scm"], V1["exv"], V1["sst"]
    aTa, aTb, vwa, vwb = V1["aTa"], V1["aTb"], V1["vwa"], V1["vwb"]
    abf = V1["abf"]
    guS, dwS, hS, sgS = V2["guS"], V2["dwS"], V2["hS"], V2["sgS"]
    xe, xeT, stg = V2["xe"], V2["xeT"], V2["stg"]

    pP = [nc.alloc_psum_tensor(f"pP{k}", [128, 512], F32) for k in range(8)]
    pB = [Buf() for _ in range(8)]

    def pbf(k):
        return pP[k].bitcast(BF16)

    bxnT = [Buf() for _ in range(CT)]
    bacc = [Buf() for _ in range(NTT)]
    bconst = Buf()
    bcmb = [Buf() for _ in range(NTT)]
    brr = [[Buf() for _ in range(12)] for _ in range(NTT)]
    bxt = [Buf() for _ in range(2)]
    bxns = [Buf() for _ in range(2)]
    bbno = [Buf() for _ in range(2)]
    bmv = [Buf() for _ in range(2)]
    blns = [Buf() for _ in range(2)]
    bqkv = [Buf() for _ in range(1)]
    bprj = [Buf() for _ in range(6)]
    bQ, bK, bV = Buf(), Buf(), Buf()
    baTa = [Buf() for _ in range(4)]
    baTb = [Buf() for _ in range(4)]
    bvwa = [Buf() for _ in range(4)]
    bvwb = [Buf() for _ in range(4)]
    bscm = [Buf() for _ in range(3)]
    babf = [Buf() for _ in range(4)]
    bexv = [Buf() for _ in range(4)]
    bsst = [Buf() for _ in range(4)]
    bahp = [Buf() for _ in range(8)]
    bgu = [Buf() for _ in range(NGU)]
    bdw = [Buf() for _ in range(NDW)]
    bh = [Buf() for _ in range(max(IT, SIT))]
    bsg = [Buf() for _ in range(2)]
    bxe = [Buf() for _ in range(2)]
    bxeT = Buf()
    bstg = [Buf() for _ in range(2)]
    bxtm = Buf()
    bxdram = Buf()
    bddram = Buf()
    bcastD = Buf()
    bcastS = Buf()
    bcastE = [Buf() for _ in range(8)]
    brk = [[Buf() for _ in range(16)] for _ in range(NTT)]

    LNATTN_BUFS = [bQ, bK, bV] + bxt + bxns + bqkv + bprj + baTa + baTb + bvwa + \
        bvwb + bscm + bexv + bsst + bahp + bbno + bmv + blns
    MOE_BUFS = bgu + bdw + bh + bsg + bxe + [bxeT] + bstg

    op("gpsimd", lambda g: g.dma_start(mcs[:, :, :], mcd.rearrange("m p w -> p m w")), wr=[bconst], dma="dC")
    op("gpsimd", lambda g: g.dma_start(mas[:, :, :], mad.rearrange("m p w -> p m w")), wr=[bconst], dma="dC")
    op("gpsimd", lambda g: g.dma_start(idb[:, :], idd[:, :]), wr=[bconst], dma="dC")
    op("gpsimd", lambda g: g.dma_start(idf[:, :], idd[:, :]), wr=[bconst], dma="dC")
    op("gpsimd", lambda g: g.dma_start(ltriS[:, :], ltrid[:, :]), wr=[bconst], dma="dC")
    op("gpsimd", lambda g: g.dma_start(iotaS[:, :], iotad[:, :]), wr=[bconst], dma="dC")
    op("gpsimd", lambda g: g.memset(alloneS[:, :], 1.0), wr=[bconst])
    op("gpsimd", lambda g: g.dma_start(pbrow[0:1, :], pbd[None, :]), wr=[bconst], dma="dC")
    op("gpsimd", lambda g: g.dma_start(sdbrow[0:1, :], sdbd[None, :]), wr=[bconst], dma="dC")
    op("gpsimd", lambda g: g.dma_start(sgbC[:, :], sgbd.rearrange("(t p) -> p t", p=128),
                                       allow_slow_non_contiguous=True), wr=[bconst], dma="dC")
    op("gpsimd", lambda g: g.dma_start(subC[:, :], subd.rearrange("(t p) -> p t", p=128),
                                       allow_slow_non_contiguous=True), wr=[bconst], dma="dC")
    op("gpsimd", lambda g: g.dma_start(rwS[:, :, :], rw.rearrange("(ct p) e -> p ct e", p=128)),
       wr=[bconst], dma="dC")
    op("gpsimd", lambda g: g.memset(epsb[:, :], 1e-5), wr=[bconst])
    op("gpsimd", lambda g: g.memset(ones1[:, :], 1.0), wr=[bconst])

    op("vector", lambda e: e.memset(xe[:, 0, :], 0.0), wr=[bxe[0]])
    for r0 in range(0, NE * CAP, 128):
        op("sync", lambda e, r0=r0: e.dma_start(
            xdram[r0:r0 + 128, :], xe[:, 0, :]), rd=[bxe[0]], wr=[bxdram], dma="dZ")
    b.fence([bxe[0], bxdram], LNATTN_BUFS)

    for grp in range(1 if cfg.get("dbg3") else 2):
        s_g = 0 if grp == 0 else GT - 8
        own0 = 0 if grp == 0 else 8

        if grp > 0:
            b.fence(MOE_BUFS, LNATTN_BUFS)

        def ln_stats(lt):
            rows = 128 if lt < NLT - 1 else REAL - 128 * (NLT - 1)
            t0 = s_g + 128 * lt
            p = lt % 2
            op("sync", lambda e, p=p, t0=t0, rows=rows: e.dma_start(
                xt_[0:rows, p, :], x[t0:t0 + rows, :]), wr=[bxt[p]], dma=f"dX{p}")
            for k in range(NBN):
                cl = min(512, C - 512 * k)
                op("vector", lambda e, p=p, rows=rows, k=k, cl=cl: e.bn_stats(
                    bno[0:rows, p, 6 * k:6 * k + 6], xt_[0:rows, p, 512 * k:512 * k + cl]),
                    rd=[bxt[p]], wr=[bbno[p]])
            op("vector", lambda e, p=p, rows=rows: e.bn_aggr(
                mv[0:rows, p, :], bno[0:rows, p, :]), rd=[bbno[p]], wr=[bmv[p]])
            op("scalar", lambda e, p=p, rows=rows: e.activation(
                lns[0:rows, p, 0:1], mv[0:rows, p, 1:2], AF.Sqrt, bias=epsb[0:rows, 0:1]),
                rd=[bmv[p], bconst], wr=[blns[p]])
            op("vector", lambda e, p=p, rows=rows: e.reciprocal(
                lns[0:rows, p, 1:2], lns[0:rows, p, 0:1]), rd=[blns[p]], wr=[blns[p]])
            op("vector", lambda e, p=p, rows=rows: e.tensor_scalar(
                lns[0:rows, p, 2:3], mv[0:rows, p, 0:1], lns[0:rows, p, 1:2], -1.0,
                ALU.mult, ALU.mult), rd=[bmv[p]], wr=[blns[p]])
            op("scalar", lambda e, p=p, rows=rows: e.activation(
                xns[0:rows, p, :], xt_[0:rows, p, :], AF.Identity,
                bias=lns[0:rows, p, 2:3], scale=lns[0:rows, p, 1:2]),
                rd=[blns[p], bxt[p]], wr=[bxns[p]])
            op("sync", lambda e, p=p, lt=lt, rows=rows: e.dma_start(
                xtm[128 * lt:128 * lt + rows, :], xns[0:rows, p, :]),
                rd=[bxns[p]], wr=[bxtm], dma="dXT")

        def ln_trans(lt):
            rows = 128 if lt < NLT - 1 else REAL - 128 * (NLT - 1)
            p = lt % 2
            col0 = 128 * lt
            for ct in range(CT):
                kb = 6 + (ct % 2)
                op("tensor", lambda e, p=p, ct=ct, rows=rows, kb=kb: e.transpose(
                    pbf(kb)[0:128, 0:rows], xns[0:rows, p, 128 * ct:128 * ct + 128],
                    idb[0:rows, 0:rows]),
                    rd=[bxns[p], bconst], war=[pB[kb]], wr=[pB[kb]])
                op("vector", lambda e, ct=ct, rows=rows, kb=kb, col0=col0: e.tensor_copy(
                    xnT[:, ct, col0:col0 + rows], pbf(kb)[0:128, 0:rows]),
                    rd=[pB[kb]], wr=[bxnT[ct]])

        for tt in range(NTT):
            t0 = grp * GT + 128 * tt
            op("scalar", lambda e, tt=tt, t0=t0: e.dma_start(
                accT[:, tt, :], x[t0:t0 + 128, :]), wr=[bacc[tt]], dma=f"dA{tt}")
        for lt in range(NLT + 1):
            if lt < NLT:
                ln_stats(lt)
            if lt >= 1:
                ln_trans(lt - 1)
        for ct in range(CT):
            op("vector", lambda e, ct=ct: e.memset(xnT[:, ct, REAL:GW], 0.0), wr=[bxnT[ct]])

        if cfg.get("dbg5") and grp == 0:
            op("sync", lambda e_: e_.dma_start(
                gdbg[:, 4096:4096 + 512], xnT[:, 0, 0:512]), rd=[bxnT[0]], dma="dHH")
        def emit_routing():
            for tt in range(NTT):
                kb = tt % 2
                oc0 = own0 + 128 * tt
                for ct in range(CT):
                    last = ct == CT - 1
                    op("tensor", lambda e, ct=ct, kb=kb, oc0=oc0: e.matmul(
                        pP[kb][:, 0:NE], xnT[:, ct, oc0:oc0 + 128], rwS[:, ct, :],
                        start=(ct == 0), stop=(ct == CT - 1)),
                        rd=[bxnT[ct], bconst] if last else (),
                        wr=[pB[kb]] if last else (),
                        rd_nt=() if last else (bxnT[ct], bconst),
                        war=(pB[kb],) if ct == 0 else (),
                        inc=last)
                op("vector", lambda e, tt=tt, kb=kb: e.tensor_copy(
                    rsc[:, tt, 0:NE], pP[kb][:, 0:NE]), rd=[pB[kb]], wr=[brr[tt][0]])
            E = NE

            def rr(tt, j):
                return rsc[:, tt, j * E:(j + 1) * E]

            def rs(tt, j):
                return rsc[:, tt, 8 * E + j:8 * E + j + 1]
            for tt in range(NTT):
                op("vector", lambda e, tt=tt: e.reduce_max(
                    rs(tt, 0), rr(tt, 0), axis=AX.X, negate=True), rd=[brr[tt][0]], wr=[brr[tt][1]])
            for tt in range(NTT):
                op("vector", lambda e, tt=tt: e.tensor_scalar_add(
                    rr(tt, 1), rr(tt, 0), rs(tt, 0)), rd=[brr[tt][0], brr[tt][1]], wr=[brr[tt][2]])
            for tt in range(NTT):
                op("scalar", lambda e, tt=tt: e.activation(
                    rr(tt, 2), rr(tt, 1), AF.Exp), rd=[brr[tt][2]], wr=[brr[tt][3]])
            for tt in range(NTT):
                op("vector", lambda e, tt=tt: e.tensor_scalar(
                    rr(tt, 3), rr(tt, 1), 0.0, None, ALU.is_equal), rd=[brr[tt][2]], wr=[brr[tt][4]])
            for tt in range(NTT):
                op("vector", lambda e, tt=tt: e.scalar_tensor_tensor(
                    rr(tt, 4), rr(tt, 3), -1e30, rr(tt, 1), ALU.mult, ALU.add),
                    rd=[brr[tt][2], brr[tt][4]], wr=[brr[tt][5]])
            for tt in range(NTT):
                op("vector", lambda e, tt=tt: e.reduce_max(
                    rs(tt, 1), rr(tt, 4), axis=AX.X), rd=[brr[tt][5]], wr=[brr[tt][6]])
            for tt in range(NTT):
                op("vector", lambda e, tt=tt: e.tensor_scalar(
                    rr(tt, 5), rr(tt, 1), rs(tt, 1), None, ALU.is_equal),
                    rd=[brr[tt][2], brr[tt][6]], wr=[brr[tt][7]])
            for tt in range(NTT):
                op("vector", lambda e, tt=tt: e.tensor_add(
                    rr(tt, 6), rr(tt, 3), rr(tt, 5)), rd=[brr[tt][4], brr[tt][7]], wr=[brr[tt][8]])
            for tt in range(NTT):
                op("vector", lambda e, tt=tt: e.tensor_mul(
                    rr(tt, 7), rr(tt, 2), rr(tt, 6)), rd=[brr[tt][3], brr[tt][8]], wr=[brr[tt][9]])
            for tt in range(NTT):
                op("vector", lambda e, tt=tt: e.reduce_sum(
                    rs(tt, 2), rr(tt, 7), axis=AX.X), rd=[brr[tt][9]], wr=[brr[tt][10]])
            for tt in range(NTT):
                op("vector", lambda e, tt=tt: e.reciprocal(
                    rs(tt, 3), rs(tt, 2)), rd=[brr[tt][10]], wr=[brr[tt][11]])
            for tt in range(NTT):
                op("vector", lambda e, tt=tt: e.tensor_scalar_mul(
                    cmbT[:, tt, :], rr(tt, 7), rs(tt, 3)),
                    rd=[brr[tt][9], brr[tt][11]], wr=[bcmb[tt]])

            for tt in range(NTT):
                op("vector", lambda e, tt=tt: e.tensor_copy(
                    indb[:, tt, :], rr(tt, 6)), rd=[brr[tt][8]], wr=[brk[tt][0]])
            for tt in range(NTT):
                kb = 2 + tt % 2
                for tt2 in range(tt):
                    op("tensor", lambda e, tt2=tt2, kb=kb, first=(tt2 == 0): e.matmul(
                        pP[kb][:, 0:NE], alloneS[:, :], indb[:, tt2, :],
                        start=first, stop=False),
                        rd_nt=(bconst, brk[tt2][0]),
                        war=(pB[kb],) if tt2 == 0 else (), inc=False)
                op("tensor", lambda e, tt=tt, kb=kb, first=(tt == 0): e.matmul(
                    pP[kb][:, 0:NE], ltriS[:, :], indb[:, tt, :],
                    start=first, stop=True),
                    rd=[bconst] + [brk[t2][0] for t2 in range(tt + 1)],
                    war=(pB[kb],) if tt == 0 else (), wr=[pB[kb]])
                op("vector", lambda e, tt=tt, kb=kb: e.tensor_copy(
                    rank8[:, tt, :], pP[kb][:, 0:NE]), rd=[pB[kb]], wr=[brk[tt][1]])
            BIGOFF = float(NE * CAP * 16)
            for tt in range(NTT):
                for j in range(2):
                    EQ = rr(tt, 3) if j == 0 else rr(tt, 5)
                    EQb = brr[tt][4] if j == 0 else brr[tt][7]
                    op("vector", lambda e, tt=tt, EQ=EQ: e.tensor_mul(
                        rtmp[:, tt, :], EQ, iotaS[:, :]), rd=[EQb, bconst], wr=[brk[tt][2]])
                    op("vector", lambda e, tt=tt, j=j: e.reduce_sum(
                        rsx[:, tt, 0 + j:1 + j], rtmp[:, tt, :], axis=AX.X),
                        rd=[brk[tt][2]], wr=[brk[tt][3 + j]])
                    op("vector", lambda e, tt=tt, EQ=EQ: e.tensor_mul(
                        rtmp[:, tt, :], EQ, rank8[:, tt, :]),
                        rd=[EQb, brk[tt][1]], wr=[brk[tt][2]])
                    op("vector", lambda e, tt=tt, j=j: e.reduce_sum(
                        rsx[:, tt, 2 + j:3 + j], rtmp[:, tt, :], axis=AX.X),
                        rd=[brk[tt][2]], wr=[brk[tt][5 + j]])
                    op("vector", lambda e, tt=tt, EQ=EQ: e.tensor_mul(
                        rtmp[:, tt, :], EQ, cmbT[:, tt, :]),
                        rd=[EQb, bcmb[tt]], wr=[brk[tt][2]])
                    op("vector", lambda e, tt=tt, j=j: e.reduce_sum(
                        c12[:, tt, j:j + 1], rtmp[:, tt, :], axis=AX.X),
                        rd=[brk[tt][2]], wr=[brk[tt][7 + j]])
                    op("vector", lambda e, tt=tt, j=j: e.tensor_scalar(
                        rsx[:, tt, 4:5], rsx[:, tt, 0 + j:1 + j], float(CAP),
                        rsx[:, tt, 2 + j:3 + j], ALU.mult, ALU.add),
                        rd=[brk[tt][3 + j], brk[tt][5 + j]], wr=[brk[tt][9]])
                    op("vector", lambda e, tt=tt: e.tensor_scalar_min(
                        rsx[:, tt, 5:6], rsx[:, tt, 4:5], float(NE * CAP - 1)),
                        rd=[brk[tt][9]], wr=[brk[tt][10]])
                    op("vector", lambda e, tt=tt, j=j: e.tensor_copy(
                        aidx[:, tt, j:j + 1], rsx[:, tt, 5:6]),
                        rd=[brk[tt][10]], wr=[brk[tt][14]])
                    op("vector", lambda e, tt=tt, j=j: e.tensor_scalar(
                        rsx[:, tt, 6:7], rsx[:, tt, 2 + j:3 + j], float(CAP), None, ALU.is_lt),
                        rd=[brk[tt][5 + j]], wr=[brk[tt][11]])
                    op("vector", lambda e, tt=tt: e.tensor_scalar_add(
                        rsx[:, tt, 7:8], rsx[:, tt, 4:5], -BIGOFF),
                        rd=[brk[tt][9]], wr=[brk[tt][12]])
                    op("vector", lambda e, tt=tt: e.tensor_mul(
                        rsx[:, tt, 7:8], rsx[:, tt, 6:7], rsx[:, tt, 7:8]),
                        rd=[brk[tt][11]], wr=[brk[tt][12]])
                    op("vector", lambda e, tt=tt: e.tensor_scalar_add(
                        rsx[:, tt, 7:8], rsx[:, tt, 7:8], BIGOFF),
                        wr=[brk[tt][12]])
                    op("vector", lambda e, tt=tt, j=j: e.tensor_copy(
                        oidx[:, tt, j:j + 1], rsx[:, tt, 7:8]),
                        rd=[brk[tt][12]], wr=[brk[tt][13]])


        GUC0 = 4
        if grp == 0:
            cast_q = [(t, e, i) for e in range(NE) for t, n_ in
                      (("g", 16), ("u", 16), ("d", IT)) if (t == "d" or e < GUC0)
                      for i in range(n_ if t != "g" or True else n_)]
            cast_q = []
            for e in range(NE):
                if e < GUC0:
                    cast_q += [("g", e, i) for i in range(CT)]
                    cast_q += [("u", e, i) for i in range(CT)]
                cast_q += [("d", e, i) for i in range(IT)]
            cast_q += [("s", 0, j) for j in range(2 * SIT)]
        else:
            cast_q = []
            for e in range(GUC0, NE):
                cast_q += [("g", e, i) for i in range(CT)]
                cast_q += [("u", e, i) for i in range(CT)]
        cast_ctr = [0]
        bounce = [ar_bf[:, s:s + C] for s in (0, 2 * C, 4 * C, 5 * C)]
        bBN = [bxt[0], bxt[1], bxns[0], bxns[1]]

        def emit_casts(n):
            while n and cast_q:
                t, e, i = cast_q.pop(0)
                sl = cast_ctr[0] % 4
                cast_ctr[0] += 1
                if t == "d":
                    srcw, dstw, wid, bb, sem = (dwt[e][128 * i:128 * i + 128, :],
                        dwb[I * e + 128 * i:I * e + 128 * i + 128, :], C, bcastD, "dW0")
                elif t == "s":
                    srcw, dstw, wid, bb, sem = (sdw[128 * i:128 * i + 128, :],
                        sdwb[128 * i:128 * i + 128, :], C, bcastS, "dV0")
                elif t == "g":
                    srcw, dstw, wid, bb, sem = (gw[e][128 * i:128 * i + 128, :],
                        gwb[C * e + 128 * i:C * e + 128 * i + 128, :], I, bcastE[e], f"dU{e}")
                else:
                    srcw, dstw, wid, bb, sem = (uwt[e][128 * i:128 * i + 128, :],
                        uwb[C * e + 128 * i:C * e + 128 * i + 128, :], I, bcastE[e], f"dU{e}")
                op("gpsimd", lambda g, srcw=srcw, sl=sl, wid=wid: g.dma_start(
                    bounce[sl][:, 0:wid], srcw), wr=[bBN[sl]], dma=f"dD{sl % NDW}")
                op("sync", lambda e_, dstw=dstw, sl=sl, wid=wid: e_.dma_start(
                    dstw, bounce[sl][:, 0:wid]), rd=[bBN[sl]], wr=[bb], dma=sem)
                n -= 1

        def load_qkv(hp):
            op("gpsimd", lambda g, hp=hp: g.dma_start(
                qkvS[:, 0, :, :], qkvw[hp].rearrange("(ct p) o -> p ct o", p=128)),
                wr=[bqkv[0]], dma="dQ0")

        def load_prj(hp):
            s8 = hp % 6
            op("gpsimd", lambda g, hp=hp, s8=s8: g.dma_start(
                prjS[:, s8, :], projw[128 * hp:128 * hp + 128, :]),
                wr=[bprj[s8]], dma=f"dP{s8}")

        proj_jobs = []

        def make_proj_block(block_hps):
            with_bias = block_hps[0] == 0
            for tt in range(NTT):
                for oc in range(NCH):
                    kb = 4 + (tt * NCH + oc) % 2
                    ol = min(512, C - 512 * oc)
                    n = len(block_hps)
                    for j, hp2 in enumerate(block_hps):
                        sa = hp2 % 8
                        s6 = hp2 % 6
                        first = j == 0
                        closing = (j == n - 1) and not with_bias
                        proj_jobs.append((
                            "tensor",
                            (lambda e, tt=tt, oc=oc, ol=ol, kb=kb, sa=sa, s6=s6,
                                    first=first, closing=closing: e.matmul(
                                pP[kb][:, 0:ol], ahp[:, sa, 128 * tt:128 * tt + 128],
                                prjS[:, s6, 512 * oc:512 * oc + ol],
                                start=first, stop=closing)),
                            dict(rd=[bahp[b2 % 8] for b2 in block_hps] +
                                    [bprj[b2 % 6] for b2 in block_hps],
                                 wr=[pB[kb]]) if closing else
                            dict(rd_nt=(bahp[sa], bprj[s6]),
                                 war=(pB[kb],) if first else (), inc=False)))
                    if with_bias:
                        proj_jobs.append((
                            "tensor",
                            (lambda e, oc=oc, ol=ol, kb=kb: e.matmul(
                                pP[kb][:, 0:ol], ones1[0:1, :],
                                pbrow[0:1, 512 * oc:512 * oc + ol],
                                start=False, stop=True)),
                            dict(rd=[bahp[b2 % 8] for b2 in block_hps] +
                                    [bprj[b2 % 6] for b2 in block_hps] + [bconst],
                                 wr=[pB[kb]])))
                    proj_jobs.append((
                        "vector",
                        (lambda e, tt=tt, oc=oc, ol=ol, kb=kb: e.tensor_add(
                            accT[:, tt, 512 * oc:512 * oc + ol],
                            accT[:, tt, 512 * oc:512 * oc + ol], pP[kb][:, 0:ol])),
                        dict(rd=[pB[kb], bacc[tt]], wr=[bacc[tt]])))

        def drain_proj(nj):
            while nj and proj_jobs:
                eng, fn, kw = proj_jobs.pop(0)
                op(eng, fn, **kw)
                nj -= 1

        HPL = 0 if cfg.get("no_attn") else HP
        if HPL:
            load_qkv(0)
            load_prj(0)
        chunks_kv = [(0, 512), (512, 512), (1024, GW - 1024)]
        chunks_q = [(own0, 512), (own0 + 512, 512)]
        for hp in range(HPL):
            p = hp % 2
            for mi, (dst, bdst) in enumerate(((Qf, bQ), (Kf, bK), (Vf, bV))):
                chunks = chunks_q if mi == 0 else chunks_kv
                for ci, (c0, cl) in enumerate(chunks):
                    kb = (mi * len(chunks) + ci) % 2
                    for ct in range(CT):
                        last = ct == CT - 1
                        op("tensor", lambda e, ct=ct, mi=mi, c0=c0, cl=cl, kb=kb: e.matmul(
                            pP[kb][:, 0:cl], qkvS[:, 0, ct, 128 * mi:128 * mi + 128],
                            xnT[:, ct, c0:c0 + cl], start=(ct == 0), stop=(ct == CT - 1)),
                            rd=[bxnT[ct], bqkv[0]] if last else (),
                            wr=[pB[kb]] if last else (),
                            rd_nt=() if last else (bxnT[ct], bqkv[0]),
                            war=(pB[kb],) if ct == 0 else (),
                            inc=last)
                    op("vector", lambda e, dst=dst, c0=c0, cl=cl, kb=kb: e.tensor_copy(
                        dst[:, c0:c0 + cl], pP[kb][:, 0:cl]), rd=[pB[kb]], wr=[bdst])
            if hp + 1 < HPL:
                load_qkv(hp + 1)
            NU = 2 * NQT

            def uparams(k):
                qt, hh = divmod(k, 2)
                if grp == 0:
                    w0, mid = (0, 1) if qt == 0 else (128 * qt - 8, 0)
                else:
                    w0, mid = (128 * qt, 0) if qt < NQT - 1 else (128 * qt, 2)
                return qt, hh, w0, mid, own0 + 128 * qt, 64 * hh

            def emit_vw(qt):
                qt_, hh_, w0, mid, q0, d0 = uparams(2 * qt)
                tp = qt % 4
                kbt = 0 + qt % 2
                op("tensor", lambda e, w0=w0, kbt=kbt: e.transpose(
                    pbf(kbt)[:, 0:128], Vf[:, w0:w0 + 128], idb[:, :]),
                    rd=[bV, bconst], war=[pB[kbt]], wr=[pB[kbt]])
                op("tensor", lambda e, w0=w0, kbt=kbt: e.transpose(
                    pbf(kbt)[0:16, 128:256], Vf[:, w0 + 128:w0 + WIN], idb[:, :]),
                    rd=[bV, bconst], wr=[pB[kbt]])
                op("vector", lambda e, kbt=kbt, tp=tp: e.tensor_copy(
                    vwa[:, tp, :], pbf(kbt)[:, 0:128]), rd=[pB[kbt]], wr=[bvwa[tp]])
                op("vector", lambda e, kbt=kbt, tp=tp: e.tensor_copy(
                    vwb[0:16, tp, :], pbf(kbt)[0:16, 128:256]), rd=[pB[kbt]], wr=[bvwb[tp]])

            def emit_scores(k):
                qt, hh, w0, mid, q0, d0 = uparams(k)
                kbs = 2 + k % 2
                op("tensor", lambda e, q0=q0, w0=w0, d0=d0, kbs=kbs: e.matmul(
                    pP[kbs][:, 0:WIN], Qf[d0:d0 + 64, q0:q0 + 128],
                    Kf[d0:d0 + 64, w0:w0 + WIN], start=True, stop=True),
                    rd=[bQ, bK], war=[pB[kbs]], wr=[pB[kbs]])

            def emit_mask(k):
                qt, hh, w0, mid, q0, d0 = uparams(k)
                kbs = 2 + k % 2
                u = k % 4
                u3 = k % 3
                op("vector", lambda e, u3=u3, kbs=kbs, mid=mid: e.tensor_mul(
                    scm[:, u3, :], pP[kbs][:, 0:WIN], mcs[:, mid, :]),
                    rd=[pB[kbs], bconst], wr=[bscm[u3]])
                op("vector", lambda e, u3=u3, mid=mid: e.tensor_add(
                    scm[:, u3, :], scm[:, u3, :], mas[:, mid, :]),
                    rd=[bconst], wr=[bscm[u3]])
                op("vector", lambda e, u=u, u3=u3: e.reduce_max(
                    sst[:, u, 0:1], scm[:, u3, :], axis=AX.X, negate=True),
                    rd=[bscm[u3]], wr=[bsst[u]])

            def emit_exp(k):
                u = k % 4
                u3 = k % 3
                op("scalar", lambda e, u=u, u3=u3: e.activation(
                    exv[:, u, :], scm[:, u3, :], AF.Exp, bias=sst[:, u, 0:1],
                    accum_out=sst[:, u, 1:2]), rd=[bscm[u3], bsst[u]], wr=[bexv[u], bsst[u]])

            def emit_scale(k):
                u = k % 4
                op("vector", lambda e, u=u: e.reciprocal(
                    sst[:, u, 2:3], sst[:, u, 1:2]), rd=[bsst[u]], wr=[bsst[u]])
                op("vector", lambda e, u=u: e.tensor_scalar_mul(
                    abf[:, u, :], exv[:, u, :], sst[:, u, 2:3]),
                    rd=[bsst[u], bexv[u]], wr=[babf[u]])

            def emit_trans(k):
                u = k % 4
                kb2 = 6 + k % 2
                op("tensor", lambda e, u=u, kb2=kb2: e.transpose(
                    pbf(kb2)[:, 0:128], abf[:, u, 0:128], idb[:, :]),
                    rd=[babf[u], bconst], war=[pB[kb2]], wr=[pB[kb2]])
                op("tensor", lambda e, u=u, kb2=kb2: e.transpose(
                    pbf(kb2)[0:16, 128:256], abf[:, u, 128:WIN], idb[:, :]),
                    rd=[babf[u], bconst], wr=[pB[kb2]])
                op("vector", lambda e, u=u, kb2=kb2: e.tensor_copy(
                    aTa[:, u, :], pbf(kb2)[:, 0:128]), rd=[pB[kb2]], wr=[baTa[u]])
                op("vector", lambda e, u=u, kb2=kb2: e.tensor_copy(
                    aTb[0:16, u, :], pbf(kb2)[0:16, 128:256]), rd=[pB[kb2]], wr=[baTb[u]])

            def emit_av(k, p):
                qt, hh, w0, mid, q0, d0 = uparams(k)
                u = k % 4
                kba = 6 + k % 2
                tp = qt % 4
                s8 = hp % 8
                op("tensor", lambda e, u=u, kba=kba, d0=d0, tp=tp: e.matmul(
                    pP[kba][0:64, 256:384], vwa[:, tp, d0:d0 + 64], aTa[:, u, :],
                    start=True, stop=False),
                    rd_nt=(bvwa[tp], baTa[u]), war=(pB[kba],), inc=False)
                op("tensor", lambda e, u=u, kba=kba, d0=d0, tp=tp: e.matmul(
                    pP[kba][0:64, 256:384], vwb[0:16, tp, d0:d0 + 64], aTb[0:16, u, :],
                    start=False, stop=True),
                    rd=[bvwa[tp], bvwb[tp], baTa[u], baTb[u]], wr=[pB[kba]])
                op("vector", lambda e, u=u, kba=kba, hh=hh, qt=qt, s8=s8: e.tensor_copy(
                    ahp[64 * hh:64 * hh + 64, s8, 128 * qt:128 * qt + 128],
                    pP[kba][0:64, 256:384]), rd=[pB[kba]], wr=[bahp[s8]])

            for step in range(NU + 6):
                if 6 <= step <= NU + 5:
                    emit_av(step - 6, p)
                if 5 <= step <= NU + 4:
                    emit_trans(step - 5)
                if 3 <= step <= NU + 2:
                    emit_scale(step - 3)
                if 2 <= step <= NU + 1:
                    emit_exp(step - 2)
                if 1 <= step <= NU:
                    emit_mask(step - 1)
                if step < NU:
                    if step % 2 == 0:
                        emit_vw(step // 2)
                    emit_scores(step)
                drain_proj(4)
            if hp % 4 == 3 or hp == HPL - 1:
                make_proj_block(list(range(4 * (hp // 4), hp + 1)))
            if hp + 1 < HPL:
                load_prj(hp + 1)
            emit_casts(15 if grp == 0 else 8)
        drain_proj(len(proj_jobs))
        emit_casts(len(cast_q))
        emit_routing()

        if cfg.get("no_moe"):
            for tt in range(NTT):
                t0 = grp * GT + 128 * tt
                op("sync", lambda e_, tt=tt, t0=t0: e_.dma_start(
                    y[t0:t0 + 128, :], accT[:, tt, :]), rd=[bacc[tt]], dma=f"dO{tt}")
            b.fence(LNATTN_BUFS, LNATTN_BUFS)
            continue
        if cfg.get("dbg5") and grp == 0:
            op("sync", lambda e_: e_.dma_start(
                gdbg[:, 4608:4608 + 512], xnT[:, 0, 0:512]), rd=[bxnT[0]], dma="dHI")
        b.fence(LNATTN_BUFS, MOE_BUFS)
        QC = C // 4
        for tt in range(NTT):
            kx = tt % 2
            op("sync", lambda e_, tt=tt, kx=kx, own0=own0: e_.dma_start(
                xe[:, kx, :], xtm[own0 + 128 * tt:own0 + 128 * tt + 128, :]),
                rd=[bxtm], wr=[bxe[kx]], dma=f"dXE{kx}")
            for j in range(2):
                op("gpsimd", lambda g, tt=tt, j=j, kx=kx: g.indirect_dma_start(
                    xdram[:, :], bass.IndirectOffsetOnAxis(oidx[:, tt, j:j + 1], 0),
                    xe[:, kx, :], None,
                    bounds_check=NE * CAP - 1, oob_is_err=False),
                    rd=[bxe[kx], brk[tt][13]], wr=[bxdram], dma="dSC")
        if cfg.get("dbg5") and grp == 0:
            op("sync", lambda e_: e_.dma_start(
                gdbg[:, 5120:5120 + 512], xnT[:, 0, 0:512]), rd=[bxnT[0]], dma="dHJ")
        especs = [("e", e, None) for e in range(NE)] + \
                 [("s", sh, th) for sh in range(2) for th in range(2)]
        if cfg.get("dbg1"):
            especs = especs[:NE]

        def gu_dma(spec, it, pair):
            kind, e, th = spec
            gate = ()
            if kind == "e":
                if grp == 1 or e < GUC0:
                    srcg = gwb[C * e:C * e + C, 128 * it:128 * it + 128]
                    srcu = uwb[C * e:C * e + C, 128 * it:128 * it + 128]
                    gate = (bcastE[e],)
                else:
                    srcg = gw[e][:, 128 * it:128 * it + 128]
                    srcu = uwt[e][:, 128 * it:128 * it + 128]
            else:
                srcg = sgw[:, SIH * e + 128 * it:SIH * e + 128 * it + 128]
                srcu = suw[:, SIH * e + 128 * it:SIH * e + 128 * it + 128]
            s0 = (2 * pair) % NGU
            s1 = (2 * pair + 1) % NGU
            op("gpsimd", lambda g, srcg=srcg, s0=s0: g.dma_start(
                guS[:, s0, :, :], srcg.rearrange("(ct p) i -> p ct i", p=128)),
                rd=list(gate), wr=[bgu[s0]], dma=f"dG{s0}")
            op("gpsimd", lambda g, srcu=srcu, s1=s1: g.dma_start(
                guS[:, s1, :, :], srcu.rearrange("(ct p) i -> p ct i", p=128)),
                rd=list(gate), wr=[bgu[s1]], dma=f"dG{s1}")

        dwctr = [0]

        def dw_dma(spec, it):
            kind, e, th = spec
            if kind == "e":
                srcd = dwb[I * e + 128 * it:I * e + 128 * it + 128, :]
                gate = (bcastD,)
            else:
                srcd = sdwb[SIH * e + 128 * it:SIH * e + 128 * it + 128, :]
                gate = (bcastS,)
            sl = dwctr[0] % NDW
            dwctr[0] += 1
            op("gpsimd", lambda g, srcd=srcd, sl=sl: g.dma_start(
                dwS[:, sl, :], srcd), rd=list(gate), wr=[bdw[sl]], dma=f"dD{sl}")
            return sl

        gu_q = [(spec, it) for spec in especs for it in range((IT if spec[0] == "e" else SIT))]
        gu_next = 0
        for _ in range(3):
            if gu_next < len(gu_q):
                gu_dma(gu_q[gu_next][0], gu_q[gu_next][1], gu_next)
                gu_next += 1

        for si, spec in enumerate(especs):
            kind, e, th = spec
            nit = IT if kind == "e" else SIT
            if kind == "e":
                for k in range(4):
                    kx = k % 2
                    op("sync", lambda e_, e=e, k=k, kx=kx: e_.dma_start(
                        xe[:, kx, :], xdram[e * CAP + 128 * k:e * CAP + 128 * k + 128, :]),
                        rd=[bxdram], wr=[bxe[kx]], dma=f"dXE{kx}")
                    for ct in range(CT):
                        kb = 4 + ct % 2
                        op("tensor", lambda e_, ct=ct, kx=kx, kb=kb: e_.transpose(
                            pbf(kb)[:, 0:128], xe[:, kx, 128 * ct:128 * ct + 128], idb[:, :]),
                            rd=[bxe[kx], bconst], war=[pB[kb]], wr=[pB[kb]])
                        op("vector", lambda e_, ct=ct, k=k, kb=kb: e_.tensor_copy(
                            xeT[:, ct, 128 * k:128 * k + 128], pbf(kb)[:, 0:128]),
                            rd=[pB[kb]], wr=[bxeT])
            if cfg.get("dbg5") and grp == 0 and spec == ("s", 0, 0):
                op("sync", lambda e_: e_.dma_start(
                    gdbg[:, 0:CT * 128], guS[:, 0, :, :].rearrange("p ct i -> p (ct i)")),
                    rd=[bgu[0]], dma="dHE")
                op("sync", lambda e_: e_.dma_start(
                    gdbg[:, 1024:1024 + CT * 128], guS[:, 1, :, :].rearrange("p ct i -> p (ct i)")),
                    rd=[bgu[1]], dma="dHF")
                for ct in range(CT):
                    op("sync", lambda e_, ct=ct, own0=own0: e_.dma_start(
                        gdbg[:, 2048 + ct * 512:2048 + ct * 512 + 512],
                        xnT[:, ct, own0:own0 + 512]), rd=[bxnT[ct]], dma=f"dHG{ct}")
            if cfg.get("dbg5") and grp == 0 and kind == "e" and e in (0, 3, 7):
                c0d = 5632 + 512 * (0 if e == 0 else (1 if e == 3 else 2))
                op("sync", lambda e_, c0d=c0d: e_.dma_start(
                    gdbg[:, c0d:c0d + 512], xnT[:, 0, 0:512]), rd=[bxnT[0]], dma=f"dHK{e}")
            for it in range(nit):
                pair = (sum((IT if s[0] == "e" else SIT) for s in especs[:si]) + it)
                s0 = (2 * pair) % NGU
                s1 = (2 * pair + 1) % NGU
                kbg = (it % 2) * 2
                kbu = kbg + 1
                for ct in range(CT):
                    last = ct == CT - 1
                    mv_ap = (lambda e_, ct=ct, s0=s0, kbg=kbg, kind=kind, th=th, own0=own0: e_.matmul(
                        pP[kbg][:, :],
                        guS[:, s0, ct, :],
                        xeT[:, ct, :] if kind == "e" else xnT[:, ct, own0 + 512 * th:own0 + 512 * th + 512],
                        start=(ct == 0), stop=(ct == CT - 1)))
                    mvb = bxeT if kind == "e" else bxnT[ct]
                    op("tensor", mv_ap,
                        rd=[bgu[s0], mvb] if last else (),
                        wr=[pB[kbg]] if last else (),
                        rd_nt=() if last else (bgu[s0], mvb),
                        war=(pB[kbg],) if ct == 0 else (),
                        inc=last)
                for ct in range(CT):
                    last = ct == CT - 1
                    mv_ap = (lambda e_, ct=ct, s1=s1, kbu=kbu, kind=kind, th=th, own0=own0: e_.matmul(
                        pP[kbu][:, :],
                        guS[:, s1, ct, :],
                        xeT[:, ct, :] if kind == "e" else xnT[:, ct, own0 + 512 * th:own0 + 512 * th + 512],
                        start=(ct == 0), stop=(ct == CT - 1)))
                    mvb = bxeT if kind == "e" else bxnT[ct]
                    op("tensor", mv_ap,
                        rd=[bgu[s1], mvb] if last else (),
                        wr=[pB[kbu]] if last else (),
                        rd_nt=() if last else (bgu[s1], mvb),
                        war=(pB[kbu],) if ct == 0 else (),
                        inc=last)
                sp = it % 2
                if kind == "e":
                    op("scalar", lambda e_, sp=sp, kbg=kbg: e_.activation(
                        sgS[:, sp, :], pP[kbg][:, :], AF.Silu), rd=[pB[kbg]], wr=[bsg[sp]])
                    op("vector", lambda e_, it=it, sp=sp, kbu=kbu: e_.tensor_mul(
                        hS[:, it, :], sgS[:, sp, :], pP[kbu][:, :]),
                        rd=[bsg[sp], pB[kbu]], wr=[bh[it]])
                else:
                    bidx = SIT * e + it
                    op("scalar", lambda e_, sp=sp, kbg=kbg, bidx=bidx: e_.activation(
                        sgS[:, sp, :], pP[kbg][:, :], AF.Silu, bias=sgbC[:, bidx:bidx + 1]),
                        rd=[pB[kbg], bconst], wr=[bsg[sp]])
                    op("vector", lambda e_, it=it, sp=sp, kbu=kbu, bidx=bidx: e_.scalar_tensor_tensor(
                        hS[:, it, :], pP[kbu][:, :],
                        subC[:, bidx:bidx + 1], sgS[:, sp, :], ALU.add, ALU.mult),
                        rd=[bsg[sp], pB[kbu], bconst], wr=[bh[it]])
                if gu_next < len(gu_q):
                    gu_dma(gu_q[gu_next][0], gu_q[gu_next][1], gu_next)
                    gu_next += 1
            if cfg.get("dbg5") and grp == 0 and spec == ("s", 0, 0):
                op("sync", lambda e_: e_.dma_start(
                    hdbg[:, 0:SIT * 512], hS[:, 0:SIT, :]), rd=[bh[i] for i in range(SIT)], dma="dHD")
            if cfg.get("dbg5") and grp == 0 and spec == ("e", 7, None):
                op("sync", lambda e_: e_.dma_start(
                    gdbg[:, 7168:7168 + 512], xnT[:, 0, 0:512]), rd=[bxnT[0]], dma="dHL")
            bias_close = (kind == "s" and e == 0)
            for och in range(2):
                slq = [dw_dma(spec, i) for i in range(min(nit, 4))]
                for it in range(nit):
                    if it + 4 < nit:
                        slq.append(dw_dma(spec, it + 4))
                    sl = slq[it]
                    lastit = it == nit - 1
                    for k in range(4):
                        for oc in range(2):
                            col = (och * 2 + oc) * QC
                            last = lastit and not bias_close
                            trk = (k == 3 and oc == 1) or last
                            op("tensor", lambda e_, it=it, k=k, oc=oc, col=col, sl=sl, last=last: e_.matmul(
                                pP[2 * k + oc][:, 0:QC], hS[:, it, 128 * k:128 * k + 128],
                                dwS[:, sl, col:col + QC],
                                start=(it == 0), stop=last),
                                rd=[bh[it], bdw[sl]] if trk else (),
                                wr=[pB[2 * k + oc]] if last else (),
                                rd_nt=() if trk else (bh[it], bdw[sl]),
                                war=(pB[2 * k + oc],) if it == 0 else (),
                                inc=trk)
                if bias_close:
                    for k in range(4):
                        for oc in range(2):
                            col = (och * 2 + oc) * QC
                            op("tensor", lambda e_, k=k, oc=oc, col=col: e_.matmul(
                                pP[2 * k + oc][:, 0:QC], ones1[32:33, :],
                                sdbrow[0:1, col:col + QC], start=False, stop=True),
                                rd=[bconst] + [bh[i] for i in range(nit)],
                                wr=[pB[2 * k + oc]])
                for k in range(4):
                    if kind == "e":
                        kx = k % 2
                        for oc in range(2):
                            col = (och * 2 + oc) * QC
                            op("vector", lambda e_, k=k, oc=oc, col=col, kx=kx: e_.tensor_copy(
                                stg[:, kx, col:col + QC], pP[2 * k + oc][:, 0:QC]),
                                rd=[pB[2 * k + oc]], wr=[bstg[kx]])
                        op("sync", lambda e_, e=e, k=k, och=och, kx=kx: e_.dma_start(
                            ddram[e * CAP + 128 * k:e * CAP + 128 * k + 128,
                                  och * 2 * QC:och * 2 * QC + 2 * QC],
                            stg[:, kx, och * 2 * QC:och * 2 * QC + 2 * QC]),
                            rd=[bstg[kx]], wr=[bddram], dma=f"dDD{kx}")
                    else:
                        tt = th * 4 + k
                        for oc in range(2):
                            col = (och * 2 + oc) * QC
                            op("vector", lambda e_, tt=tt, k=k, oc=oc, col=col: e_.tensor_add(
                                accT[:, tt, col:col + QC],
                                accT[:, tt, col:col + QC], pP[2 * k + oc][:, 0:QC]),
                                rd=[pB[2 * k + oc], bacc[tt]], wr=[bacc[tt]])
            if cfg.get("dbg5") and grp == 0 and spec == ("e", 7, None):
                op("sync", lambda e_: e_.dma_start(
                    gdbg[:, 7680:7680 + 512], xnT[:, 0, 0:512]), rd=[bxnT[0]], dma="dHM")
            if kind == "e" and e == NE - 1 and not cfg.get("dbg2"):
                for tt in range(NTT):
                    for j in range(2):
                        kx = j % 2
                        op("gpsimd", lambda g, tt=tt, j=j, kx=kx: g.indirect_dma_start(
                            xe[:, kx, :], None, ddram[:, :],
                            bass.IndirectOffsetOnAxis(aidx[:, tt, j:j + 1], 0)),
                            rd=[bddram, brk[tt][14]], wr=[bxe[kx]], dma=f"dGA{kx}")
                        op("vector", lambda e_, tt=tt, j=j, kx=kx: e_.scalar_tensor_tensor(
                            accT[:, tt, :], xe[:, kx, :], c12[:, tt, j:j + 1],
                            accT[:, tt, :], ALU.mult, ALU.add),
                            rd=[bxe[kx], brk[tt][7 + j], bacc[tt]], wr=[bacc[tt]])
            if cfg.get("dbg5") and grp == 0 and spec == ("e", 7, None):
                op("sync", lambda e_: e_.dma_start(
                    gdbg[:, 8192:8192 + 512], xnT[:, 0, 0:512]), rd=[bxnT[0]], dma="dHN")
            if (kind == "s" and e == 1 and th == 1) or (cfg.get("dbg1") and si == len(especs) - 1):
                for tt in range(NTT):
                    t0 = grp * GT + 128 * tt
                    op("sync", lambda e_, tt=tt, t0=t0: e_.dma_start(
                        y[t0:t0 + 128, :], accT[:, tt, :]), rd=[bacc[tt]], dma=f"dO{tt}")

    fin = []
    for tt in range(NTT):
        name = f"dO{tt}"
        if name in b.sems:
            fin.append((b.sems[name], b.cnt[name]))
    b.plan["sync"].append((lambda e: e.nop(), fin, None, False))
    b.build_block()
    return b


def prep_inputs(L0_attn_x, L0_attn_qkv_w, L0_attn_proj_w, L0_attn_proj_b,
                L0_ffn_router_w, L0_ffn_gate_w, L0_ffn_up_w, L0_ffn_down_w,
                L0_ffn_s_gate_w, L0_ffn_s_gate_b, L0_ffn_s_up_w, L0_ffn_s_up_b,
                L0_ffn_s_down_w, L0_ffn_s_down_b, cfg):
    C, HP = cfg["C"], cfg["HEADS"] // 2
    qkv = np.asarray(L0_attn_qkv_w, np.float32)
    qkvw = np.empty((HP, C, 384), np.float32)
    for hp in range(HP):
        blk = np.concatenate([qkv[128 * hp:128 * hp + 128],
                              qkv[C + 128 * hp:C + 128 * hp + 128],
                              qkv[2 * C + 128 * hp:2 * C + 128 * hp + 128]], 0)
        qkvw[hp] = np.ascontiguousarray(blk.T)
    mc, ma = make_masks(C // cfg["HEADS"])
    shared = {
        "qkvw": qkvw,
        "projw": np.ascontiguousarray(np.asarray(L0_attn_proj_w, np.float32).T),
        "rw": np.ascontiguousarray(np.asarray(L0_ffn_router_w, np.float32).T),
        "gw": np.ascontiguousarray(np.asarray(L0_ffn_gate_w, np.float32)),
        "uw": np.ascontiguousarray(np.asarray(L0_ffn_up_w, np.float32)),
        "dw": np.ascontiguousarray(np.asarray(L0_ffn_down_w, np.float32)),
        "sgw": np.ascontiguousarray(np.asarray(L0_ffn_s_gate_w, np.float32)),
        "suw": np.ascontiguousarray(np.asarray(L0_ffn_s_up_w, np.float32)),
        "sdw": np.ascontiguousarray(np.asarray(L0_ffn_s_down_w, np.float32)),
        "pb": np.ascontiguousarray(np.asarray(L0_attn_proj_b, np.float32)),
        "sgb": np.ascontiguousarray(np.asarray(L0_ffn_s_gate_b, np.float32)),
        "sub": np.ascontiguousarray(np.asarray(L0_ffn_s_up_b, np.float32)),
        "sdb": np.ascontiguousarray(np.asarray(L0_ffn_s_down_b, np.float32)),
        "ident": np.eye(128, dtype=np.float32),
        "ltri": np.tril(np.ones((128, 128), np.float32), -1).T,
        "iota8": np.broadcast_to(np.arange(cfg["NE"], dtype=np.float32), (128, cfg["NE"])).copy(),
        "mc": mc,
        "ma": ma,
    }
    xs = np.ascontiguousarray(np.asarray(L0_attn_x, np.float32))
    return xs, shared


def kernel(**inputs):
    cfg = FULL_CFG
    xs, shared = prep_inputs(cfg=cfg, **inputs)
    Bn = xs.shape[0]
    bld = build_program(cfg)
    in_maps = [dict(shared, x=xs[bb]) for bb in range(Bn)]
    try:
        res = run_bass_kernel_spmd(bld.nc, in_maps, core_ids=list(range(Bn)))
    finally:
        bld.stack.close()
    return np.stack([np.asarray(r["y"], np.float32) for r in res.results], 0)
